# version 13
# baseline (speedup 1.0000x reference)
import numpy as np
import concourse.bass as bass
import concourse.mybir as mybir
from concourse.bass_utils import run_bass_kernel_spmd

F32 = mybir.dt.float32
BF16 = mybir.dt.bfloat16
AF = mybir.ActivationFunctionType
OP = mybir.AluOpType
AX = mybir.AxisListType

D = 1024
N = 32
DG = 8
I = 2816
K4 = 4
EPS = 1e-5


class Prog:

    def __init__(self, nc):
        self.nc = nc
        self.ops = []
        self.last_write = {}
        self.reads_since = {}

    @staticmethod
    def _key(buf):
        if isinstance(buf, tuple):
            return buf[1]
        t = getattr(buf, "tensor", buf)
        return t.name if hasattr(t, "name") else str(t)

    def op(self, engine, emit, reads=(), writes=(), dma=False, tag=""):
        idx = len(self.ops)
        deps = set()
        for b in reads:
            w = self.last_write.get(self._key(b))
            if w is not None:
                deps.add(w)
        for b in writes:
            k = self._key(b)
            w = self.last_write.get(k)
            if w is not None:
                if not (dma and self.ops[w]["dma"]):
                    deps.add(w)
            for r in self.reads_since.get(k, ()):
                deps.add(r)
        self.ops.append(dict(engine=engine, emit=emit, deps=sorted(deps), dma=dma,
                             tag=tag, milestone=False, mcount=None, sem=None))
        for b in reads:
            self.reads_since.setdefault(self._key(b), []).append(idx)
        for b in writes:
            k = self._key(b)
            self.last_write[k] = idx
            self.reads_since[k] = []
        return idx

    def dma(self, queue, out, in_, tag=""):
        def emit(eng):
            return eng.dma_start(out, in_)
        idx = self.op(queue, emit, reads=(in_,), writes=(out,), dma=True, tag=tag)
        self.ops[idx]["dmakey"] = self._key(out)
        return idx

    def build(self, block, sem_factory):
        ops = self.ops
        for o in ops:
            for d in o["deps"]:
                ops[d]["milestone"] = True
        counts = {}
        for o in ops:
            if o["dma"]:
                key = ("dmab", o["dmakey"])
                counts[key] = counts.get(key, 0) + 1
                o["mcount"] = counts[key] * 16
                o["sem"] = key
                o["milestone"] = True
            elif o["milestone"]:
                key = o["engine"]
                counts[key] = counts.get(key, 0) + 1
                o["mcount"] = counts[key]
                o["sem"] = key
        sems = {}
        for o in ops:
            if o.get("sem") is not None and o["sem"] not in sems:
                sems[o["sem"]] = sem_factory(o["sem"])
        per_engine = {}
        for i, o in enumerate(ops):
            per_engine.setdefault(o["engine"], []).append(i)

        def emit_stream(eng_name, eng):
            waited = {}
            for i in per_engine.get(eng_name, []):
                o = ops[i]
                need = {}
                for d in o["deps"]:
                    dep = ops[d]
                    sk = dep["sem"]
                    need[sk] = max(need.get(sk, 0), dep["mcount"])
                for sk, val in sorted(need.items(), key=lambda kv: str(kv[0])):
                    if waited.get(sk, 0) < val:
                        eng.wait_ge(sems[sk], val)
                        waited[sk] = val
                inst = o["emit"](eng)
                if o["dma"]:
                    inst.then_inc(sems[o["sem"]], 16)
                elif o["milestone"]:
                    inst.then_inc(sems[o["sem"]], 1)

        @block.sync
        def _(eng):
            emit_stream("sync", eng)

        @block.tensor
        def _(eng):
            emit_stream("tensor", eng)

        @block.vector
        def _(eng):
            emit_stream("vector", eng)

        @block.scalar
        def _(eng):
            emit_stream("scalar", eng)

        @block.gpsimd
        def _(eng):
            emit_stream("gpsimd", eng)


class MK:
    def __init__(self, B=4, L=2048, T=128):
        assert L % T == 0 and T % 128 == 0
        self.B, self.L, self.T = B, L, T
        self.NPH = L // T
        self.TPB = T // 128
        self.W = B * T
        self.W3 = B * (T + 3)
        self.nc = bass.Bass("TRN2", target_bir_lowering=False, debug=False)
        self.p = Prog(self.nc)

    def sb(self, name, shape, dtype):
        return self.nc.alloc_sbuf_tensor(name, list(shape), dtype).ap()

    def ps(self, name, shape, dtype=F32):
        return self.nc.alloc_psum_tensor(name, list(shape), dtype).ap()

    def dram(self, name, shape, dtype, kind="Internal"):
        return self.nc.dram_tensor(name, list(shape), dtype, kind=kind).ap()

    def t_mm(self, out, lhsT, rhs, start, stop, tag="mm"):
        self.p.op("tensor",
                  lambda e: self.nc.tensor.matmul(out, lhsT, rhs, start=start, stop=stop),
                  reads=(lhsT, rhs), writes=(out,), tag=tag)

    def t_tr(self, out, in_, ident, tag="tr"):
        self.p.op("tensor", lambda e: self.nc.tensor.transpose(out, in_, ident),
                  reads=(in_, ident), writes=(out,), tag=tag)

    def v(self, fn, reads, writes, tag="v"):
        self.p.op("vector", fn, reads=reads, writes=writes, tag=tag)

    def s(self, fn, reads, writes, tag="s"):
        self.p.op("scalar", fn, reads=reads, writes=writes, tag=tag)

    def g(self, fn, reads, writes, tag="g"):
        self.p.op("gpsimd", fn, reads=reads, writes=writes, tag=tag)

    def emit_quant(self, x_ap, out_ap, scr, nblk, tag="q"):
        nc = self.nc
        P = x_ap.shape[0]
        F = nblk * 16
        bm, sc, rs = scr["bm"][:, :nblk], scr["sc"][:, :nblk], scr["rs"][:, :nblk]
        ax = scr["p1"][:, :F]
        av = scr["p2"][:, :F]
        lA = scr["p3"][:, :F]
        lB = scr["v"][:, :F]
        kb, kv, k1, k2, k3 = scr["bm"], scr["v"], scr["p1"], scr["p2"], scr["p3"]
        ksc, krs = scr["sc"], scr["rs"]
        V, S, G = self.v, self.s, self.g
        S(lambda e: nc.scalar.activation(ax, x_ap, AF.Abs), (x_ap,), (k1,), tag + ":ax")
        bm3 = ax.rearrange("p (b k) -> p b k", k=16)
        V(lambda e: nc.vector.reduce_max(bm, bm3, axis=AX.X), (k1,), (kb,), tag + ":bm")
        V(lambda e: nc.vector.tensor_scalar(sc, bm, float(1.0 / 6.0), 1e-12, OP.mult, OP.max),
          (kb,), (ksc,), tag)
        V(lambda e: nc.vector.reciprocal(rs, sc), (ksc,), (krs,), tag)
        rs_b = rs.unsqueeze(2).broadcast_to([P, nblk, 16])
        ax3 = ax.rearrange("p (b k) -> p b k", k=16)
        av3 = av.rearrange("p (b k) -> p b k", k=16)
        V(lambda e: nc.vector.tensor_tensor(av3, ax3, rs_b, OP.mult), (k1, krs), (k2,), tag)
        V(lambda e: nc.vector.tensor_scalar(lA, av, 0.25, None, OP.is_gt), (k2,), (k3,), tag)
        for th in (0.75, 1.25, 1.75):
            V(lambda e, th=th: nc.vector.scalar_tensor_tensor(lA, av, th, lA, OP.is_gt, OP.add),
              (k2, k3), (k3,), tag)
        V(lambda e: nc.vector.tensor_scalar(lB, av, 2.5, None, OP.is_gt), (k2,), (kv,), tag)
        V(lambda e: nc.vector.scalar_tensor_tensor(lB, av, 3.5, lB, OP.is_gt, OP.add),
          (k2, kv), (kv,), tag)
        V(lambda e: nc.vector.tensor_scalar(av, av, 5.0, 2.0, OP.is_gt, OP.mult), (k2,), (k2,), tag)
        S(lambda e: nc.scalar.activation(ax, x_ap, AF.Sign), (x_ap,), (k1,), tag + ":sg")
        V(lambda e: nc.vector.scalar_tensor_tensor(lA, lA, 0.5, lB, OP.mult, OP.add),
          (k3, kv), (k3,), tag)
        V(lambda e: nc.vector.tensor_tensor(lA, lA, av, OP.add), (k3, k2), (k3,), tag)
        G(lambda e: nc.gpsimd.tensor_tensor(lB, lA, ax, OP.mult), (k3, k1), (kv,), tag)
        sc_b = sc.unsqueeze(2).broadcast_to([P, nblk, 16])
        q3 = lB.rearrange("p (b k) -> p b k", k=16)
        o3 = out_ap.rearrange("p (b k) -> p b k", k=16)
        V(lambda e: nc.vector.tensor_tensor(o3, q3, sc_b, OP.mult),
          (kv, ksc), (out_ap,), tag + ":dq")

    def emit_ln(self, x_ap, out_ap, scr, gamma=None, beta=None, tag="ln"):
        nc = self.nc
        P = x_ap.shape[0]
        F = x_ap.shape[-1]
        ssum, ssq = scr["st_sum"][:P], scr["st_sq"][:P]
        sm, sr = scr["st_m"][:P], scr["st_r"][:P]
        sq = scr["sq"][:P, :F]
        ks = scr
        self.v(lambda e: nc.vector.reduce_sum(ssum, x_ap, axis=AX.X), (x_ap,), (ks["st_sum"],), tag)
        self.s(lambda e: nc.scalar.activation(sq, x_ap, AF.Square, accum_out=ssq),
               (x_ap,), (ks["sq"], ks["st_sq"]), tag)
        self.v(lambda e: nc.vector.tensor_scalar(sm, ssum, 1.0 / F, None, OP.mult),
               (ks["st_sum"],), (ks["st_m"],), tag)
        self.v(lambda e: nc.vector.tensor_scalar(sr, ssq, 1.0 / F, EPS, OP.mult, OP.add),
               (ks["st_sq"],), (ks["st_r"],), tag)
        self.v(lambda e: nc.vector.scalar_tensor_tensor(ssq, sm, -1.0, sm, OP.mult, OP.mult),
               (ks["st_m"],), (ks["st_sq"],), tag)
        self.v(lambda e: nc.vector.tensor_tensor(sr, sr, ssq, OP.add),
               (ks["st_r"], ks["st_sq"]), (ks["st_r"],), tag)
        self.s(lambda e: nc.scalar.activation(sr, sr, AF.Sqrt), (ks["st_r"],), (ks["st_r"],), tag)
        self.v(lambda e: nc.vector.reciprocal(sr, sr), (ks["st_r"],), (ks["st_r"],), tag)
        self.v(lambda e: nc.vector.scalar_tensor_tensor(sm, sm, -1.0, sr, OP.mult, OP.mult),
               (ks["st_m"], ks["st_r"]), (ks["st_m"],), tag)
        self.v(lambda e: nc.vector.tensor_scalar(out_ap, x_ap, sr, sm, OP.mult, OP.add),
               (x_ap, ks["st_r"], ks["st_m"]), (out_ap,), tag + ":nrm")
        if gamma is not None:
            self.v(lambda e: nc.vector.tensor_tensor(out_ap, out_ap, gamma, OP.mult),
                   (out_ap, gamma), (out_ap,), tag)
            self.v(lambda e: nc.vector.tensor_tensor(out_ap, out_ap, beta, OP.add),
                   (out_ap, beta), (out_ap,), tag)


def build_core_program(B=4, L=2048, T=128):
    mk = MK(B, L, T)
    nc, p = mk.nc, mk.p
    W, W3, TPB, NPH = mk.W, mk.W3, mk.TPB, mk.NPH
    WCB = 1 + T
    WC4 = 4 * B * WCB
    ICW = 512
    CHUNKS = []
    _c0 = 0
    while _c0 < I:
        CHUNKS.append((_c0, min(ICW, I - _c0)))
        _c0 += ICW
    NCH = len(CHUNKS)

    x_in = mk.dram("x", (B, L, D), F32, kind="ExternalInput")
    out_t = mk.dram("out", (B, L, D), F32, kind="ExternalOutput")
    w_inT = mk.dram("w_inT", (D, 2 * D), F32, kind="ExternalInput")
    b_in = mk.dram("b_in", (2 * D,), F32, kind="ExternalInput")
    conv_w = mk.dram("conv_w", (D, K4), F32, kind="ExternalInput")
    conv_b = mk.dram("conv_b", (D,), F32, kind="ExternalInput")
    A_log = mk.dram("A_log", (D, N), F32, kind="ExternalInput")
    w_bcT = mk.dram("w_bcT", (D, 2 * N), F32, kind="ExternalInput")
    b_bc = mk.dram("b_bc", (2 * N,), F32, kind="ExternalInput")
    w_dtT = mk.dram("w_dtT", (D, D), F32, kind="ExternalInput")
    b_dt = mk.dram("b_dt", (D,), F32, kind="ExternalInput")
    w_outT = mk.dram("w_outT", (D, D), F32, kind="ExternalInput")
    b_out = mk.dram("b_out", (D,), F32, kind="ExternalInput")
    ln_g = mk.dram("ln_g", (D,), F32, kind="ExternalInput")
    ln_b = mk.dram("ln_b", (D,), F32, kind="ExternalInput")
    w_g = mk.dram("w_g", (I, D), F32, kind="ExternalInput")
    w_u = mk.dram("w_u", (I, D), F32, kind="ExternalInput")
    w_d = mk.dram("w_d", (D, I), F32, kind="ExternalInput")
    ident_in = mk.dram("ident", (128, 128), F32, kind="ExternalInput")

    bc_bounce = mk.dram("bc_bounce", (2 * N, W), BF16)
    winT_d = mk.dram("winT_d", (D, 2 * D), BF16)
    wdtT_d = mk.dram("wdtT_d", (D, D), BF16)
    woutT_d = mk.dram("woutT_d", (D, D), BF16)
    wq_gT_d = mk.dram("wq_gT_d", (D, I), BF16)
    wq_uT_d = mk.dram("wq_uT_d", (D, I), BF16)
    wq_dT_d = mk.dram("wq_dT_d", (I, D), BF16)

    ident_f = mk.sb("ident_f", (128, 128), F32)
    ident_b = mk.sb("ident_b", (128, 128), BF16)
    win_ch = [mk.sb(f"win_ch{i}", (128, 8 * 128), BF16) for i in range(2)]
    wdt_ch = win_ch
    wout_ph = mk.sb("wout_ph", (128, 8 * D), BF16)
    wbcT = mk.sb("wbcT", (128, 8 * 2 * N), BF16)
    wg_ch = mk.sb("wg_chA", (128, 8 * ICW), BF16)
    wu_ch = mk.sb("wu_chA", (128, 8 * ICW), BF16)
    wd_ch = mk.sb("wd_chA", (128, (ICW // 128) * D), BF16)
    convw = mk.sb("convw", (128, DG * K4), F32)
    convb = mk.sb("convb", (128, DG), F32)
    bdt = mk.sb("bdt", (128, DG), F32)
    bin_sb = mk.sb("bin_sb", (128, 16), F32)
    bbc = mk.sb("bbc", (64, 1), F32)
    A_sb = mk.sb("A_sb", (128, DG * N), F32)
    g_rep = mk.sb("g_rep", (128, D), BF16)
    b_rep = mk.sb("b_rep", (128, D), BF16)
    ob_rep = mk.sb("ob_rep", (128, D), BF16)
    carry = mk.sb("carry", (128, DG * N * B), BF16)

    xn_tm = mk.sb("xn_tm", (128, B * TPB * D), F32)
    xn_fm = mk.sb("xn_fm", (128, DG * W3), BF16)
    xc = mk.sb("xc", (128, W3), F32)
    xs = mk.sb("xs", (128, DG * W), BF16)
    sz = mk.sb("sz", (128, DG * W), BF16)
    dt_sb = mk.sb("dt_sb", (128, DG * W), BF16)
    dtx = mk.sb("dtx", (128, DG * W), BF16)
    bc_fm = mk.sb("bc_fm", (64, W), BF16)
    a_bufs = [mk.sb(f"a_buf{i}", (128, WC4), F32) for i in range(2)]
    u_bufs = [mk.sb(f"u_buf{i}", (128, WC4), BF16) for i in range(2)]
    V_bufs = [mk.sb(f"V_buf{i}", (128, 4 * W), BF16) for i in range(2)]
    Breps = [mk.sb(f"Brep{i}", (128, W), BF16) for i in range(2)]
    Creps = [mk.sb(f"Crep{i}", (128, W), BF16) for i in range(2)]
    Yacc = mk.sb("Yacc", (128, 4 * W), F32)
    y_fm = sz
    xq_tm = mk.sb("xq_tm", (128, D), BF16)
    SCR_A = mk.sb("SCR_A", (128, D), F32)
    q_bm = mk.sb("q_bm", (128, 64), F32)
    q_sc = mk.sb("q_sc", (128, 64), F32)
    q_rs = mk.sb("q_rs", (128, 64), F32)
    q_v = mk.sb("q_v", (128, D), F32)
    q_p1 = mk.sb("q_p1", (128, D), F32)
    q_p2 = mk.sb("q_p2", (128, D), F32)
    q_p3 = mk.sb("q_p3", (128, D), F32)
    qscr = dict(bm=q_bm, sc=q_sc, rs=q_rs, v=q_v[:], p1=q_p1[:], p2=q_p2[:], p3=q_p3[:])
    st_sum = mk.sb("st_sum", (128, 1), F32)
    st_sq = mk.sb("st_sq", (128, 1), F32)
    st_m = mk.sb("st_m", (128, 1), F32)
    st_r = mk.sb("st_r", (128, 1), F32)
    lnscr = dict(st_sum=st_sum, st_sq=st_sq, st_m=st_m, st_r=st_r, sq=q_v[:])
    h_sb = mk.sb("h_sb", (128, 512), F32)
    hq_tm = mk.sb("hq_tm", (128, 512), BF16)
    hq_fm = mk.sb("hq_fm", (128, 4 * 128), BF16)
    dacc = mk.sb("dacc", (128, (W // 128) * D), BF16)
    attn_sb = mk.sb("attn_sb", (128, D), F32)
    xt = attn_sb
    xn1 = SCR_A[0:1, :D]
    xres = q_p2[:, :D]
    o_sb = mk.sb("o_sb", (128, D), F32)
    cacc = q_v[:, :W] if W <= D else mk.sb("cacc", (128, W), F32)
    sp1 = q_p1[:, :512]
    sp2 = q_p2[:, :512]
    sp3 = q_p3[:, :512]
    hs_sb = xc[:, :512] if W3 >= 512 else mk.sb("hs_sb", (128, 512), F32)
    WS = SCR_A
    WPREP = sz[:, :2048]
    WQS = WPREP[:, :1024]
    WTS = WPREP[:, 1024:2048]

    ps1 = mk.ps("ps1", (128, 512))
    ps2 = mk.ps("ps2", (128, 512))
    ps_g = mk.ps("ps_g", (128, 512))
    ps_y = mk.ps("ps_y", (128, 512))
    ps_o = mk.ps("ps_o", (128, 1024))
    ps_tb = mk.ps("ps_tb", (128, 128), BF16)

    V, S, G, TMM, TTR = mk.v, mk.s, mk.g, mk.t_mm, mk.t_tr
    dma = p.dma

    dma("sync", ident_f[:], ident_in[:], "prep")
    V(lambda e: nc.vector.tensor_copy(ident_b[:], ident_f[:]), (ident_f,), (ident_b,), "prep")

    def cast_to_bounce(src, dst, ncols, tag):
        for kk in range(8):
            c0 = 0
            while c0 < ncols:
                cw = min(D, ncols - c0)
                stv = WS[:, :cw]
                dma("sync", stv, src[kk * 128:(kk + 1) * 128, c0:c0 + cw], tag)
                qv = WQS[:, :cw]
                V(lambda e, qv=qv, stv=stv: nc.vector.tensor_copy(qv, stv), (SCR_A,), (sz,), tag)
                dma("scalar", dst[kk * 128:(kk + 1) * 128, c0:c0 + cw], qv, tag)
                c0 += cw

    cast_to_bounce(w_inT, winT_d, 2 * D, "prep:win")
    cast_to_bounce(w_dtT, wdtT_d, D, "prep:wdt")
    cast_to_bounce(w_outT, woutT_d, D, "prep:wout")

    for kk in range(8):
        stv = WS[:, :2 * N]
        dma("sync", stv, w_bcT[kk * 128:(kk + 1) * 128, :], "prep:wbc")
        dstv = wbcT[:, kk * 2 * N:(kk + 1) * 2 * N]
        V(lambda e, dstv=dstv, stv=stv: nc.vector.tensor_copy(dstv, stv), (SCR_A,), (wbcT,), "prep")

    for dg in range(DG):
        dma("sync", convw[:, dg * K4:(dg + 1) * K4], conv_w[dg * 128:(dg + 1) * 128, :], "prep")
        dma("sync", convb[:, dg:dg + 1], conv_b[dg * 128:(dg + 1) * 128].unsqueeze(1), "prep")
        dma("sync", bdt[:, dg:dg + 1], b_dt[dg * 128:(dg + 1) * 128].unsqueeze(1), "prep")
        stv = WS[:, :N]
        dma("sync", stv, A_log[dg * 128:(dg + 1) * 128, :], "prep")
        a_d = A_sb[:, dg * N:(dg + 1) * N]
        V(lambda e, a_d=a_d, stv=stv: nc.vector.tensor_scalar(a_d, stv, -5.0, 2.0, OP.max, OP.min),
          (SCR_A,), (A_sb,), "prep")
        S(lambda e, a_d=a_d: nc.scalar.activation(a_d, a_d, AF.Exp), (A_sb,), (A_sb,), "prep")
        V(lambda e, a_d=a_d: nc.vector.tensor_scalar(a_d, a_d, -1.0, None, OP.mult),
          (A_sb,), (A_sb,), "prep")
    for jg in range(16):
        dma("sync", bin_sb[:, jg:jg + 1], b_in[jg * 128:(jg + 1) * 128].unsqueeze(1), "prep")
    dma("sync", bbc[:], b_bc[:].unsqueeze(1), "prep")
    for src_d, dst_sb in ((ln_g, g_rep), (ln_b, b_rep), (b_out, ob_rep)):
        stv = WS[:, :D]
        dma("sync", stv, src_d[:].unsqueeze(0).broadcast_to([128, D]), "prep")
        V(lambda e, dst_sb=dst_sb, stv=stv: nc.vector.tensor_copy(dst_sb[:], stv),
          (SCR_A,), (dst_sb,), "prep")
    G(lambda e: nc.gpsimd.memset(carry[:], 0.0), (),
      tuple([(carry, "carry%d" % (i * 4 * B)) for i in range(2 * N)]), "prep")
    G(lambda e: nc.gpsimd.memset(a_bufs[0][:], 0.0), (), (a_bufs[0],), "prep")
    G(lambda e: nc.gpsimd.memset(a_bufs[1][:], 0.0), (), (a_bufs[1],), "prep")

    def prep_ffn_weight(w_src, wq_dst, nrows, ncols, tag):
        for rg in range(nrows // 128):
            c0 = 0
            while c0 < ncols:
                cw = min(D, ncols - c0)
                stv = WS[:, :cw]
                dma("sync", stv, w_src[rg * 128:(rg + 1) * 128, c0:c0 + cw], tag)
                qd = WQS[:, :cw]
                mk.emit_quant(stv, qd, qscr, cw // 16, tag)
                for it in range(cw // 128):
                    TTR(ps_tb[:], qd[:, it * 128:(it + 1) * 128], ident_b[:], tag)
                    dtile = WTS[:, it * 128:(it + 1) * 128]
                    V(lambda e, dtile=dtile: nc.vector.tensor_copy(dtile, ps_tb[:]),
                      (ps_tb,), (sz,), tag)
                o = wq_dst[c0:c0 + cw, rg * 128:(rg + 1) * 128].rearrange(
                    "(it p) c -> p it c", p=128)
                i_ = WTS[:, :cw].rearrange("p (it c) -> p it c", c=128)
                dma("scalar", o, i_, tag)
                c0 += cw

    prep_ffn_weight(w_g, wq_gT_d, I, D, "prep:wg")
    prep_ffn_weight(w_u, wq_uT_d, I, D, "prep:wu")
    prep_ffn_weight(w_d, wq_dT_d, D, I, "prep:wd")

    NTG = W // 128

    for ph in range(NPH):
        t0 = ph * T
        tagp = f"p{ph}"

        if ph > 0:
            for dg in range(DG):
                w3v = xn_fm[:, dg * W3:(dg + 1) * W3].rearrange("p (b c) -> p b c", c=T + 3)
                src = w3v[:, :, T:T + 2]
                dst = w3v[:, :, 0:2]
                V(lambda e, dst=dst, src=src: nc.vector.tensor_copy(dst, src),
                  (xn_fm,), (xn_fm,), tagp + ":tail")
        else:
            G(lambda e: nc.gpsimd.memset(xn_fm[:], 0.0), (), (xn_fm,), tagp)

        for b in range(B):
            for j in range(TPB):
                ta = t0 + j * 128
                dma("sync", xt[:], x_in[b, ta:ta + 128, :], tagp + ":x")
                xnt = xn_tm[:, (b * TPB + j) * D:(b * TPB + j + 1) * D]
                mk.emit_ln(xt[:], xnt, lnscr, tag=tagp + ":ln")
                for dg in range(DG):
                    TTR(ps1[:, 0:128], xnt[:, dg * 128:(dg + 1) * 128], ident_f[:], tagp + ":tr")
                    c0 = dg * W3 + b * (T + 3) + 2 + j * 128
                    dcol = xn_fm[:, c0:c0 + 128]
                    V(lambda e, dcol=dcol: nc.vector.tensor_copy(dcol, ps1[:, 0:128]),
                      (ps1,), (xn_fm,), tagp + ":tre")
            if ph < NPH - 1:
                dma("sync", xt[0:1, :], x_in[b, t0 + T:t0 + T + 1, :], tagp + ":x1")
                mk.emit_ln(xt[0:1, :], xn1[:], lnscr, tag=tagp + ":ln1")
                for dg in range(DG):
                    TTR(ps1[0:128, 0:1], xn1[:, dg * 128:(dg + 1) * 128], ident_f[0:1, 0:1], tagp + ":tr1")
                    c0 = dg * W3 + b * (T + 3) + 2 + T
                    dcol = xn_fm[:, c0:c0 + 1]
                    V(lambda e, dcol=dcol: nc.vector.tensor_copy(dcol, ps1[:, 0:1]),
                      (ps1,), (xn_fm,), tagp + ":tre1")

        for jg in range(16):
            is_xc = jg < 8
            dg = jg % 8
            wch = win_ch[jg % 2]
            wsrc = winT_d[:, jg * 128:(jg + 1) * 128].rearrange("(kk p) c -> p kk c", p=128)
            dma("sync", wch[:].rearrange("p (kk c) -> p kk c", c=128), wsrc, tagp + ":winl")
            bias = bin_sb[:, jg:jg + 1]
            if is_xc:
                for b in range(B):
                    pp = (ps1, ps2)[b % 2]
                    for kk in range(8):
                        lhs = wch[:, kk * 128:(kk + 1) * 128]
                        rhs = xn_fm[:, kk * W3 + b * (T + 3):kk * W3 + (b + 1) * (T + 3)]
                        TMM(pp[:, 0:T + 3], lhs, rhs, start=(kk == 0), stop=(kk == 7), tag=tagp + ":inp")
                    dst = xc[:, b * (T + 3):(b + 1) * (T + 3)]
                    V(lambda e, dst=dst, pp=pp, bias=bias:
                      nc.vector.tensor_scalar(dst, pp[:, 0:T + 3], bias, None, OP.add),
                      (pp, bin_sb, wch), (xc,), tagp + ":xce")
                if ph == 0:
                    for b in range(B):
                        pad = xc[:, b * (T + 3):b * (T + 3) + 2]
                        V(lambda e, pad=pad: nc.vector.memset(pad, 0.0), (), (xc,), tagp)
                if ph == NPH - 1:
                    for b in range(B):
                        pad = xc[:, b * (T + 3) + T + 2:b * (T + 3) + T + 3]
                        V(lambda e, pad=pad: nc.vector.memset(pad, 0.0), (), (xc,), tagp)
                cv = cacc[:].rearrange("p (b c) -> p b c", c=T)
                wv = convw[:, dg * K4:dg * K4 + 1]
                x_sh = xc[:].rearrange("p (b c) -> p b c", c=T + 3)[:, :, 0:T]
                V(lambda e, cv=cv, x_sh=x_sh, wv=wv:
                  nc.vector.tensor_scalar(cv, x_sh, wv, None, OP.mult),
                  (xc, convw), (cacc,), tagp + ":cv0")
                for k in range(1, K4):
                    wv = convw[:, dg * K4 + k:dg * K4 + k + 1]
                    x_sh = xc[:].rearrange("p (b c) -> p b c", c=T + 3)[:, :, k:k + T]
                    V(lambda e, cv=cv, x_sh=x_sh, wv=wv:
                      nc.vector.scalar_tensor_tensor(cv, x_sh, wv, cv, OP.mult, OP.add),
                      (xc, convw, cacc), (cacc,), tagp + f":cv{k}")
                xs_d = xs[:, dg * W:(dg + 1) * W]
                cb = convb[:, dg:dg + 1]
                S(lambda e, xs_d=xs_d, cb=cb:
                  nc.scalar.activation(xs_d, cacc[:], AF.Silu, bias=cb),
                  (cacc, convb), (xs,), tagp + ":xs")
            else:
                for b in range(B):
                    pp = (ps1, ps2)[b % 2]
                    for kk in range(8):
                        lhs = wch[:, kk * 128:(kk + 1) * 128]
                        rhs = xn_fm[:, kk * W3 + b * (T + 3):kk * W3 + (b + 1) * (T + 3)]
                        TMM(pp[:, 0:T + 3], lhs, rhs, start=(kk == 0), stop=(kk == 7), tag=tagp + ":zmm")
                    dst = sz[:, dg * W + b * T:dg * W + (b + 1) * T]
                    S(lambda e, dst=dst, pp=pp, bias=bias:
                      nc.scalar.activation(dst, pp[:, 2:2 + T], AF.Silu, bias=bias),
                      (pp, bin_sb, wch), (sz,), tagp + ":z")

        for dgo in range(DG):
            wch = wdt_ch[dgo % 2]
            wsrc = wdtT_d[:, dgo * 128:(dgo + 1) * 128].rearrange("(kk p) c -> p kk c", p=128)
            dma("sync", wch[:].rearrange("p (kk c) -> p kk c", c=128), wsrc, tagp + ":wdtl")
            pp = (ps1, ps2)[dgo % 2]
            for kk in range(8):
                lhs = wch[:, kk * 128:(kk + 1) * 128]
                rhs = xs[:, kk * W:kk * W + W]
                TMM(pp[:, 0:W], lhs, rhs, start=(kk == 0), stop=(kk == 7), tag=tagp + ":dtm")
            bias = bdt[:, dgo:dgo + 1]
            V(lambda e, pp=pp, bias=bias:
              nc.vector.tensor_scalar(sp1[:, :W], pp[:, :W], bias, 0.0, OP.add, OP.max),
              (pp, bdt), (sp1,), tagp)
            V(lambda e, pp=pp, bias=bias:
              nc.vector.tensor_scalar(sp3[:, :W], pp[:, :W], bias, None, OP.add),
              (pp, bdt), (sp3,), tagp)
            V(lambda e: nc.vector.scalar_tensor_tensor(sp2[:, :W], sp1[:, :W], -2.0, sp3[:, :W], OP.mult, OP.add),
              (sp1, sp3), (sp2,), tagp)
            S(lambda e: nc.scalar.activation(sp2[:, :W], sp2[:, :W], AF.Exp), (sp2,), (sp2,), tagp)
            S(lambda e: nc.scalar.activation(sp2[:, :W], sp2[:, :W], AF.Ln, bias=1.0), (sp2,), (sp2,), tagp)
            V(lambda e: nc.vector.tensor_tensor(sp1[:, :W], sp1[:, :W], sp2[:, :W], OP.add),
              (sp1, sp2), (sp1,), tagp)
            dtd = dt_sb[:, dgo * W:(dgo + 1) * W]
            V(lambda e, dtd=dtd: nc.vector.tensor_scalar(dtd, sp1[:, :W], 0.2, 0.001, OP.min, OP.max),
              (sp1, wch), (dt_sb,), tagp + ":dt")
            xsd = xs[:, dgo * W:(dgo + 1) * W]
            dxd = dtx[:, dgo * W:(dgo + 1) * W]
            V(lambda e, dxd=dxd, dtd=dtd, xsd=xsd: nc.vector.tensor_tensor(dxd, dtd, xsd, OP.mult),
              (dt_sb, xs), (dtx,), tagp + ":dtx")
        pp = ps1
        for kk in range(8):
            lhs = wbcT[:, kk * 2 * N:(kk + 1) * 2 * N]
            rhs = xs[:, kk * W:kk * W + W]
            TMM(pp[0:64, 0:W], lhs, rhs, start=(kk == 0), stop=(kk == 7), tag=tagp + ":bcm")
        V(lambda e: nc.vector.tensor_scalar(bc_fm[:, :W], pp[0:64, :W], bbc[:], None, OP.add),
          (pp, bbc), (bc_fm,), tagp + ":bce")
        dma("scalar", bc_bounce[:], bc_fm[:], tagp + ":bcb")

        for tg in range(NTG):
            xnt = xn_tm[:, tg * D:(tg + 1) * D]
            mk.emit_quant(xnt, xq_tm[:], qscr, D // 16, tagp + ":xq")
            for dg in range(DG):
                TTR(ps_tb[:], xq_tm[:, dg * 128:(dg + 1) * 128], ident_b[:], tagp + ":qtr")
                dcol = xs[:, dg * W + tg * 128:dg * W + (tg + 1) * 128]
                V(lambda e, dcol=dcol: nc.vector.tensor_copy(dcol, ps_tb[:]),
                  (ps_tb,), (xs,), tagp + ":qtre")
        xq_fm = xs

        seg = lambda buf: buf[:].rearrange("p (g b c) -> p g b c", b=B, c=WCB)

        def emit_uc(half, n):
            pi = n % 2
            u_buf = u_bufs[pi]
            u_C = seg(u_buf)[:, :, :, 0:1]
            cidx = (n * 2 + half) * 4 * B
            cv4 = carry[:, cidx:cidx + 4 * B].rearrange("p (g b o) -> p g b o", b=B, o=1)
            G(lambda e, cv4=cv4, u_C=u_C: nc.gpsimd.tensor_copy(u_C, cv4),
              ((carry, "carry%d" % cidx),), ((u_bufs[pi], "uc%d" % pi),), tagp + ":uc")

        def d_pre(half):
            V(lambda e: nc.vector.memset(Yacc[:], 0.0), (), (Yacc,), tagp + ":ya0")
            emit_uc(half, 0)
            emit_uc(half, 1)

        def d_n(half, n):
            pi = n % 2
            a_buf, V_buf, Brep, Crep = a_bufs[pi], V_bufs[pi], Breps[pi], Creps[pi]
            u_buf = u_bufs[pi]
            ukm = ("um%d" % pi)
            ukc = ("uc%d" % pi)
            a_T = seg(a_buf)[:, :, :, 1:WCB]
            u_T = seg(u_buf)[:, :, :, 1:WCB]
            s_T = u_T
            s_L = seg(u_buf)[:, :, :, T:T + 1]
            cidx = (n * 2 + half) * 4 * B
            cv4 = carry[:, cidx:cidx + 4 * B].rearrange("p (g b o) -> p g b o", b=B, o=1)
            dma("sync", Brep[:], bc_bounce[n:n + 1, :].broadcast_to([128, W]), tagp + ":Br")
            dma("sync", Crep[:], bc_bounce[N + n:N + n + 1, :].broadcast_to([128, W]), tagp + ":Cr")
            for dgi in range(4):
                dg = half * 4 + dgi
                dtd = dt_sb[:, dg * W:(dg + 1) * W].rearrange("p (b t) -> p b t", t=T)
                aT1 = a_T[:, dgi]
                S(lambda e, dtd=dtd, dg=dg, n=n, aT1=aT1:
                  nc.scalar.activation(aT1, dtd, AF.Exp, scale=A_sb[:, dg * N + n:dg * N + n + 1]),
                  (dt_sb, A_sb), (a_buf,), tagp + ":a")
            dxh = dtx[:, half * 4 * W:(half * 4 + 4) * W].rearrange("p (g b t) -> p g b t", b=B, t=T)
            br4 = Brep[:].rearrange("p (b t) -> p b t", t=T).unsqueeze(1).broadcast_to([128, 4, B, T])
            V(lambda e, dxh=dxh, br4=br4, u_T=u_T: nc.vector.tensor_tensor(u_T, dxh, br4, OP.mult),
              (dtx, Brep), ((u_buf, ukm),), tagp + ":u")
            V(lambda e, a_buf=a_buf, u_buf=u_buf: nc.vector.tensor_tensor_scan(u_buf[:], a_buf[:], u_buf[:], 0.0, OP.mult, OP.add),
              (a_buf, (u_buf, ukm), (u_buf, ukc)), ((u_buf, ukm), (u_buf, ukc)), tagp + ":tts")
            G(lambda e, cv4=cv4, s_L=s_L: nc.gpsimd.tensor_copy(cv4, s_L),
              (((u_buf, ukm)),), ((carry, "carry%d" % cidx),), tagp + ":cs")
            if n + 2 < N:
                emit_uc(half, n + 2)
            cr4 = Crep[:].rearrange("p (b t) -> p b t", t=T).unsqueeze(1).broadcast_to([128, 4, B, T])
            Vv = V_buf[:].rearrange("p (g b t) -> p g b t", b=B, t=T)
            V(lambda e, cr4=cr4, Vv=Vv, s_T=s_T: nc.vector.tensor_tensor(Vv, s_T, cr4, OP.mult),
              (((u_buf, ukm)), Crep), (V_buf,), tagp + ":V")
            G(lambda e, V_buf=V_buf: nc.gpsimd.tensor_tensor(Yacc[:], Yacc[:], V_buf[:], OP.add),
              (Yacc, V_buf), (Yacc,), tagp + ":ya")

        def d_post(half):
            for dgi in range(4):
                dg = half * 4 + dgi
                yv = Yacc[:, dgi * W:(dgi + 1) * W]
                szd = sz[:, dg * W:(dg + 1) * W]
                V(lambda e, yv=yv, szd=szd: nc.vector.tensor_tensor(szd, yv, szd, OP.mult),
                  (Yacc, sz), (sz,), tagp + ":y")

        d_units = []
        for half in range(2):
            d_units.append(lambda half=half: d_pre(half))
            for n in range(N):
                d_units.append(lambda half=half, n=n: d_n(half, n))
            d_units.append(lambda half=half: d_post(half))

        def f_load(ci):
            ic0, icw = CHUNKS[ci]
            gsrc = wq_gT_d[:, ic0:ic0 + icw].rearrange("(kk p) c -> p kk c", p=128)
            dma("scalar", wg_ch[:, :8 * icw].rearrange("p (kk c) -> p kk c", c=icw), gsrc, tagp + ":wgl")
            usrc = wq_uT_d[:, ic0:ic0 + icw].rearrange("(kk p) c -> p kk c", p=128)
            dma("scalar", wu_ch[:, :8 * icw].rearrange("p (kk c) -> p kk c", c=icw), usrc, tagp + ":wul")
            dsrc = wq_dT_d[ic0:ic0 + icw, :].rearrange("(it p) c -> p it c", p=128)
            dma("scalar", wd_ch[:, :(icw // 128) * D].rearrange("p (it c) -> p it c", c=D), dsrc, tagp + ":wdl")

        def f_dacc0():
            G(lambda e: nc.gpsimd.memset(dacc[:], 0.0), (), (dacc,), tagp + ":da0")

        def f_comp(ci, tg):
            ic0, icw = CHUNKS[ci]
            nit = icw // 128
            for kk in range(8):
                lhs = xq_fm[:, kk * W + tg * 128:kk * W + (tg + 1) * 128]
                TMM(ps_g[:, :icw], lhs, wg_ch[:, kk * icw:(kk + 1) * icw],
                    start=(kk == 0), stop=(kk == 7), tag=tagp + ":gm")
            for kk in range(8):
                lhs = xq_fm[:, kk * W + tg * 128:kk * W + (tg + 1) * 128]
                TMM(ps_y[:, :icw], lhs, wu_ch[:, kk * icw:(kk + 1) * icw],
                    start=(kk == 0), stop=(kk == 7), tag=tagp + ":um")
            S(lambda e, icw=icw: nc.scalar.activation(hs_sb[:, :icw], ps_g[:, :icw], AF.Silu),
              (ps_g,), (hs_sb,), tagp + ":hs")
            V(lambda e, icw=icw: nc.vector.tensor_tensor(h_sb[:, :icw], hs_sb[:, :icw], ps_y[:, :icw], OP.mult),
              (hs_sb, ps_y, wg_ch, wu_ch), (h_sb,), tagp + ":h")
            mk.emit_quant(h_sb[:, :icw], hq_tm[:, :icw], qscr, icw // 16, tagp + ":hq")
            for it in range(nit):
                TTR(ps_tb[:], hq_tm[:, it * 128:(it + 1) * 128], ident_b[:], tagp + ":htr")
                dcol = hq_fm[:, it * 128:(it + 1) * 128]
                V(lambda e, dcol=dcol: nc.vector.tensor_copy(dcol, ps_tb[:]),
                  (ps_tb,), (hq_fm,), tagp + ":htre")
            for it in range(nit):
                for jc in range(2):
                    TMM(ps_o[:, jc * 512:(jc + 1) * 512], hq_fm[:, it * 128:(it + 1) * 128],
                        wd_ch[:, it * D + jc * 512:it * D + (jc + 1) * 512],
                        start=(it == 0), stop=(it == nit - 1), tag=tagp + ":dm")
            dv = dacc[:, tg * D:(tg + 1) * D]
            V(lambda e, dv=dv: nc.vector.tensor_tensor(dv, dv, ps_o[:], OP.add),
              (dacc, ps_o, wd_ch), (dacc,), tagp + ":dacc")

        def f_woutl():
            wsrc = woutT_d[:, :].rearrange("(kk p) c -> p kk c", p=128)
            dma("scalar", wout_ph[:].rearrange("p (kk c) -> p kk c", c=D), wsrc, tagp + ":wol")

        f_units = [lambda: f_load(0), f_dacc0, f_woutl]
        for ci in range(NCH):
            for tg in range(NTG):
                f_units.append(lambda ci=ci, tg=tg: f_comp(ci, tg))
            if ci + 1 < NCH:
                f_units.append(lambda ci=ci: f_load(ci + 1))

        INTERLEAVE = True
        if INTERLEAVE:
            npre = 3
            for k in range(npre):
                f_units[k]()
            f_done = npre
            nf, nd = len(f_units), len(d_units)
            for di, du in enumerate(d_units):
                du()
                want = npre + (di + 1) * (nf - npre) // nd
                while f_done < want:
                    f_units[f_done]()
                    f_done += 1
            while f_done < nf:
                f_units[f_done]()
                f_done += 1
        else:
            for du in d_units:
                du()
            for fu in f_units:
                fu()

        for tg in range(NTG):
            b, j = tg // TPB, tg % TPB
            for kk in range(DG):
                lhs = y_fm[:, kk * W + tg * 128:kk * W + (tg + 1) * 128]
                for jc in range(2):
                    TMM(ps_o[:, jc * 512:(jc + 1) * 512], lhs,
                        wout_ph[:, kk * D + jc * 512:kk * D + (jc + 1) * 512],
                        start=(kk == 0), stop=(kk == 7), tag=tagp + ":om")
            xnt = xn_tm[:, tg * D:(tg + 1) * D]
            V(lambda e: nc.vector.tensor_tensor(attn_sb[:], ps_o[:], ob_rep[:], OP.add),
              (ps_o, ob_rep, wout_ph), (attn_sb,), tagp + ":ob")
            V(lambda e, xnt=xnt: nc.vector.tensor_tensor(attn_sb[:], attn_sb[:], xnt, OP.add),
              (attn_sb, xn_tm), (attn_sb,), tagp + ":resm")
            mk.emit_ln(attn_sb[:], attn_sb[:], lnscr, gamma=g_rep[:], beta=b_rep[:], tag=tagp + ":mln")
            ta = t0 + j * 128
            dma("sync", xres[:], x_in[b, ta:ta + 128, :], tagp + ":xr")
            V(lambda e: nc.vector.tensor_tensor(o_sb[:], attn_sb[:], xres[:], OP.add),
              (attn_sb, q_p2), (o_sb,), tagp + ":fin1")
            dv = dacc[:, tg * D:(tg + 1) * D]
            V(lambda e, dv=dv: nc.vector.tensor_tensor(o_sb[:], o_sb[:], dv, OP.add),
              (o_sb, dacc), (o_sb,), tagp + ":fin2")
            dma("scalar", out_t[b, ta:ta + 128, :], o_sb[:], tagp + ":out")

    V(lambda e: nc.vector.engine_nop(), (out_t,), (), "final")

    from contextlib import ExitStack
    with ExitStack() as es:
        _sem_n = [0]

        def sem_factory(key):
            _sem_n[0] += 1
            return es.enter_context(nc.semaphore(f"sm{_sem_n[0]}"))

        block = es.enter_context(nc.Block())
        p.build(block, sem_factory)
    nc.finalize()
    return nc


_NC_CACHE = {}


def _get_nc(B, L, T):
    key = (B, L, T)
    if key not in _NC_CACHE:
        _NC_CACHE[key] = build_core_program(B, L, T)
    return _NC_CACHE[key]


def make_in_map(inputs, xshard):
    c = lambda a: np.ascontiguousarray(np.asarray(a), dtype=np.float32)
    return dict(
        x=c(xshard),
        w_inT=c(np.asarray(inputs["L0_attn_in_proj_w"]).T),
        b_in=c(inputs["L0_attn_in_proj_b"]),
        conv_w=c(np.asarray(inputs["L0_attn_conv_w"]).reshape(D, K4)),
        conv_b=c(inputs["L0_attn_conv_b"]),
        A_log=c(inputs["L0_attn_A_log"]),
        w_bcT=c(np.concatenate([np.asarray(inputs["L0_attn_B_w"]).T,
                                np.asarray(inputs["L0_attn_C_w"]).T], axis=1)),
        b_bc=c(np.concatenate([np.asarray(inputs["L0_attn_B_b"]),
                               np.asarray(inputs["L0_attn_C_b"])])),
        w_dtT=c(np.asarray(inputs["L0_attn_dt_w"]).T),
        b_dt=c(inputs["L0_attn_dt_b"]),
        w_outT=c(np.asarray(inputs["L0_attn_out_w"]).T),
        b_out=c(inputs["L0_attn_out_b"]),
        ln_g=c(inputs["L0_attn_ln_g"]),
        ln_b=c(inputs["L0_attn_ln_beta"]),
        w_g=c(inputs["L0_ffn_w_gate"]),
        w_u=c(inputs["L0_ffn_w_up"]),
        w_d=c(inputs["L0_ffn_w_down"]),
        ident=np.eye(128, dtype=np.float32),
    )


def kernel(**inputs):
    x = np.asarray(inputs["L0_attn_x"])
    NB = x.shape[0]
    ncores = 8
    bpc = NB // ncores
    nc = _get_nc(bpc, x.shape[1], 128)
    in_maps = [make_in_map(inputs, x[i * bpc:(i + 1) * bpc]) for i in range(ncores)]
    res = run_bass_kernel_spmd(nc, in_maps, core_ids=list(range(ncores)))
    out = np.concatenate([np.asarray(r["out"]) for r in res.results], axis=0)
    return out.astype(np.float32)


# version 15
# speedup vs baseline: 1.0356x; 1.0356x over previous
import numpy as np
import concourse.bass as bass
import concourse.mybir as mybir
from concourse.bass_utils import run_bass_kernel_spmd

F32 = mybir.dt.float32
BF16 = mybir.dt.bfloat16
AF = mybir.ActivationFunctionType
OP = mybir.AluOpType
AX = mybir.AxisListType

D = 1024
N = 32
DG = 8
I = 2816
K4 = 4
EPS = 1e-5


class Prog:

    def __init__(self, nc):
        self.nc = nc
        self.ops = []
        self.last_write = {}
        self.reads_since = {}

    @staticmethod
    def _key(buf):
        if isinstance(buf, tuple):
            return buf[1]
        t = getattr(buf, "tensor", buf)
        return t.name if hasattr(t, "name") else str(t)

    def op(self, engine, emit, reads=(), writes=(), dma=False, tag=""):
        idx = len(self.ops)
        deps = set()
        for b in reads:
            w = self.last_write.get(self._key(b))
            if w is not None:
                deps.add(w)
        for b in writes:
            k = self._key(b)
            w = self.last_write.get(k)
            if w is not None:
                if not (dma and self.ops[w]["dma"]):
                    deps.add(w)
            for r in self.reads_since.get(k, ()):
                deps.add(r)
        self.ops.append(dict(engine=engine, emit=emit, deps=sorted(deps), dma=dma,
                             tag=tag, milestone=False, mcount=None, sem=None))
        for b in reads:
            self.reads_since.setdefault(self._key(b), []).append(idx)
        for b in writes:
            k = self._key(b)
            self.last_write[k] = idx
            self.reads_since[k] = []
        return idx

    def dma(self, queue, out, in_, tag=""):
        def emit(eng):
            return eng.dma_start(out, in_)
        idx = self.op(queue, emit, reads=(in_,), writes=(out,), dma=True, tag=tag)
        self.ops[idx]["dmakey"] = self._key(out)
        return idx

    def build(self, block, sem_factory):
        ops = self.ops
        for o in ops:
            for d in o["deps"]:
                ops[d]["milestone"] = True
        counts = {}
        for o in ops:
            if o["dma"]:
                key = ("dmab", o["dmakey"])
                counts[key] = counts.get(key, 0) + 1
                o["mcount"] = counts[key] * 16
                o["sem"] = key
                o["milestone"] = True
            elif o["milestone"]:
                key = o["engine"]
                counts[key] = counts.get(key, 0) + 1
                o["mcount"] = counts[key]
                o["sem"] = key
        sems = {}
        for o in ops:
            if o.get("sem") is not None and o["sem"] not in sems:
                sems[o["sem"]] = sem_factory(o["sem"])
        per_engine = {}
        for i, o in enumerate(ops):
            per_engine.setdefault(o["engine"], []).append(i)

        def emit_stream(eng_name, eng):
            waited = {}
            for i in per_engine.get(eng_name, []):
                o = ops[i]
                need = {}
                for d in o["deps"]:
                    dep = ops[d]
                    sk = dep["sem"]
                    need[sk] = max(need.get(sk, 0), dep["mcount"])
                for sk, val in sorted(need.items(), key=lambda kv: str(kv[0])):
                    if waited.get(sk, 0) < val:
                        eng.wait_ge(sems[sk], val)
                        waited[sk] = val
                inst = o["emit"](eng)
                if o["dma"]:
                    inst.then_inc(sems[o["sem"]], 16)
                elif o["milestone"]:
                    inst.then_inc(sems[o["sem"]], 1)

        @block.sync
        def _(eng):
            emit_stream("sync", eng)

        @block.tensor
        def _(eng):
            emit_stream("tensor", eng)

        @block.vector
        def _(eng):
            emit_stream("vector", eng)

        @block.scalar
        def _(eng):
            emit_stream("scalar", eng)

        @block.gpsimd
        def _(eng):
            emit_stream("gpsimd", eng)


class MK:
    def __init__(self, B=4, L=2048, T=128):
        assert L % T == 0 and T % 128 == 0
        self.B, self.L, self.T = B, L, T
        self.NPH = L // T
        self.TPB = T // 128
        self.W = B * T
        self.W3 = B * (T + 3)
        self.nc = bass.Bass("TRN2", target_bir_lowering=False, debug=False)
        self.p = Prog(self.nc)

    def sb(self, name, shape, dtype):
        return self.nc.alloc_sbuf_tensor(name, list(shape), dtype).ap()

    def ps(self, name, shape, dtype=F32):
        return self.nc.alloc_psum_tensor(name, list(shape), dtype).ap()

    def dram(self, name, shape, dtype, kind="Internal"):
        return self.nc.dram_tensor(name, list(shape), dtype, kind=kind).ap()

    def t_mm(self, out, lhsT, rhs, start, stop, tag="mm"):
        self.p.op("tensor",
                  lambda e: self.nc.tensor.matmul(out, lhsT, rhs, start=start, stop=stop),
                  reads=(lhsT, rhs), writes=(out,), tag=tag)

    def t_tr(self, out, in_, ident, tag="tr"):
        self.p.op("tensor", lambda e: self.nc.tensor.transpose(out, in_, ident),
                  reads=(in_, ident), writes=(out,), tag=tag)

    def v(self, fn, reads, writes, tag="v"):
        self.p.op("vector", fn, reads=reads, writes=writes, tag=tag)

    def s(self, fn, reads, writes, tag="s"):
        self.p.op("scalar", fn, reads=reads, writes=writes, tag=tag)

    def g(self, fn, reads, writes, tag="g"):
        self.p.op("gpsimd", fn, reads=reads, writes=writes, tag=tag)

    def emit_quant(self, x_ap, out_ap, scr, nblk, tag="q"):
        nc = self.nc
        P = x_ap.shape[0]
        F = nblk * 16
        bm, sc, rs = scr["bm"][:, :nblk], scr["sc"][:, :nblk], scr["rs"][:, :nblk]
        ax = scr["p1"][:, :F]
        av = scr["p2"][:, :F]
        lA = scr["p3"][:, :F]
        lB = scr["v"][:, :F]
        kb, kv, k1, k2, k3 = scr["bm"], scr["v"], scr["p1"], scr["p2"], scr["p3"]
        ksc, krs = scr["sc"], scr["rs"]
        V, S, G = self.v, self.s, self.g
        S(lambda e: nc.scalar.activation(ax, x_ap, AF.Abs), (x_ap,), (k1,), tag + ":ax")
        bm3 = ax.rearrange("p (b k) -> p b k", k=16)
        V(lambda e: nc.vector.reduce_max(bm, bm3, axis=AX.X), (k1,), (kb,), tag + ":bm")
        V(lambda e: nc.vector.tensor_scalar(sc, bm, float(1.0 / 6.0), 1e-12, OP.mult, OP.max),
          (kb,), (ksc,), tag)
        V(lambda e: nc.vector.reciprocal(rs, sc), (ksc,), (krs,), tag)
        rs_b = rs.unsqueeze(2).broadcast_to([P, nblk, 16])
        ax3 = ax.rearrange("p (b k) -> p b k", k=16)
        av3 = av.rearrange("p (b k) -> p b k", k=16)
        V(lambda e: nc.vector.tensor_tensor(av3, ax3, rs_b, OP.mult), (k1, krs), (k2,), tag)
        V(lambda e: nc.vector.tensor_scalar(lA, av, 0.25, None, OP.is_gt), (k2,), (k3,), tag)
        for th in (0.75, 1.25, 1.75):
            V(lambda e, th=th: nc.vector.scalar_tensor_tensor(lA, av, th, lA, OP.is_gt, OP.add),
              (k2, k3), (k3,), tag)
        V(lambda e: nc.vector.tensor_scalar(lB, av, 2.5, None, OP.is_gt), (k2,), (kv,), tag)
        V(lambda e: nc.vector.scalar_tensor_tensor(lB, av, 3.5, lB, OP.is_gt, OP.add),
          (k2, kv), (kv,), tag)
        V(lambda e: nc.vector.tensor_scalar(av, av, 5.0, 2.0, OP.is_gt, OP.mult), (k2,), (k2,), tag)
        S(lambda e: nc.scalar.activation(ax, x_ap, AF.Sign), (x_ap,), (k1,), tag + ":sg")
        V(lambda e: nc.vector.scalar_tensor_tensor(lA, lA, 0.5, lB, OP.mult, OP.add),
          (k3, kv), (k3,), tag)
        V(lambda e: nc.vector.tensor_tensor(lA, lA, av, OP.add), (k3, k2), (k3,), tag)
        G(lambda e: nc.gpsimd.tensor_tensor(lB, lA, ax, OP.mult), (k3, k1), (kv,), tag)
        sc_b = sc.unsqueeze(2).broadcast_to([P, nblk, 16])
        q3 = lB.rearrange("p (b k) -> p b k", k=16)
        o3 = out_ap.rearrange("p (b k) -> p b k", k=16)
        V(lambda e: nc.vector.tensor_tensor(o3, q3, sc_b, OP.mult),
          (kv, ksc), (out_ap,), tag + ":dq")

    def emit_ln(self, x_ap, out_ap, scr, gamma=None, beta=None, tag="ln"):
        nc = self.nc
        P = x_ap.shape[0]
        F = x_ap.shape[-1]
        ssum, ssq = scr["st_sum"][:P], scr["st_sq"][:P]
        sm, sr = scr["st_m"][:P], scr["st_r"][:P]
        sq = scr["sq"][:P, :F]
        ks = scr
        self.v(lambda e: nc.vector.reduce_sum(ssum, x_ap, axis=AX.X), (x_ap,), (ks["st_sum"],), tag)
        self.s(lambda e: nc.scalar.activation(sq, x_ap, AF.Square, accum_out=ssq),
               (x_ap,), (ks["sq"], ks["st_sq"]), tag)
        self.v(lambda e: nc.vector.tensor_scalar(sm, ssum, 1.0 / F, None, OP.mult),
               (ks["st_sum"],), (ks["st_m"],), tag)
        self.v(lambda e: nc.vector.tensor_scalar(sr, ssq, 1.0 / F, EPS, OP.mult, OP.add),
               (ks["st_sq"],), (ks["st_r"],), tag)
        self.v(lambda e: nc.vector.scalar_tensor_tensor(ssq, sm, -1.0, sm, OP.mult, OP.mult),
               (ks["st_m"],), (ks["st_sq"],), tag)
        self.v(lambda e: nc.vector.tensor_tensor(sr, sr, ssq, OP.add),
               (ks["st_r"], ks["st_sq"]), (ks["st_r"],), tag)
        self.s(lambda e: nc.scalar.activation(sr, sr, AF.Sqrt), (ks["st_r"],), (ks["st_r"],), tag)
        self.v(lambda e: nc.vector.reciprocal(sr, sr), (ks["st_r"],), (ks["st_r"],), tag)
        self.v(lambda e: nc.vector.scalar_tensor_tensor(sm, sm, -1.0, sr, OP.mult, OP.mult),
               (ks["st_m"], ks["st_r"]), (ks["st_m"],), tag)
        self.v(lambda e: nc.vector.tensor_scalar(out_ap, x_ap, sr, sm, OP.mult, OP.add),
               (x_ap, ks["st_r"], ks["st_m"]), (out_ap,), tag + ":nrm")
        if gamma is not None:
            self.v(lambda e: nc.vector.tensor_tensor(out_ap, out_ap, gamma, OP.mult),
                   (out_ap, gamma), (out_ap,), tag)
            self.v(lambda e: nc.vector.tensor_tensor(out_ap, out_ap, beta, OP.add),
                   (out_ap, beta), (out_ap,), tag)


def build_core_program(B=4, L=2048, T=128):
    mk = MK(B, L, T)
    nc, p = mk.nc, mk.p
    W, W3, TPB, NPH = mk.W, mk.W3, mk.TPB, mk.NPH
    WCB = 1 + T
    WC4 = 4 * B * WCB
    ICW = 512
    CHUNKS = []
    _c0 = 0
    while _c0 < I:
        CHUNKS.append((_c0, min(ICW, I - _c0)))
        _c0 += ICW
    NCH = len(CHUNKS)

    x_in = mk.dram("x", (B, L, D), F32, kind="ExternalInput")
    out_t = mk.dram("out", (B, L, D), F32, kind="ExternalOutput")
    w_inT = mk.dram("w_inT", (D, 2 * D), F32, kind="ExternalInput")
    b_in = mk.dram("b_in", (2 * D,), F32, kind="ExternalInput")
    conv_w = mk.dram("conv_w", (D, K4), F32, kind="ExternalInput")
    conv_b = mk.dram("conv_b", (D,), F32, kind="ExternalInput")
    A_log = mk.dram("A_log", (D, N), F32, kind="ExternalInput")
    w_bcT = mk.dram("w_bcT", (D, 2 * N), F32, kind="ExternalInput")
    b_bc = mk.dram("b_bc", (2 * N,), F32, kind="ExternalInput")
    w_dtT = mk.dram("w_dtT", (D, D), F32, kind="ExternalInput")
    b_dt = mk.dram("b_dt", (D,), F32, kind="ExternalInput")
    w_outT = mk.dram("w_outT", (D, D), F32, kind="ExternalInput")
    b_out = mk.dram("b_out", (D,), F32, kind="ExternalInput")
    ln_g = mk.dram("ln_g", (D,), F32, kind="ExternalInput")
    ln_b = mk.dram("ln_b", (D,), F32, kind="ExternalInput")
    w_g = mk.dram("w_g", (I, D), F32, kind="ExternalInput")
    w_u = mk.dram("w_u", (I, D), F32, kind="ExternalInput")
    w_d = mk.dram("w_d", (D, I), F32, kind="ExternalInput")
    ident_in = mk.dram("ident", (128, 128), F32, kind="ExternalInput")

    bc_bounce = mk.dram("bc_bounce", (2 * N, W), BF16)
    winT_d = mk.dram("winT_d", (D, 2 * D), BF16)
    wdtT_d = mk.dram("wdtT_d", (D, D), BF16)
    woutT_d = mk.dram("woutT_d", (D, D), BF16)
    wq_gT_d = mk.dram("wq_gT_d", (D, I), BF16)
    wq_uT_d = mk.dram("wq_uT_d", (D, I), BF16)
    wq_dT_d = mk.dram("wq_dT_d", (I, D), BF16)

    ident_f = mk.sb("ident_f", (128, 128), F32)
    ident_b = mk.sb("ident_b", (128, 128), BF16)
    win_ch = [mk.sb(f"win_ch{i}", (128, 8 * 128), BF16) for i in range(2)]
    wdt_ch = win_ch
    wout_ph = mk.sb("wout_ph", (128, 8 * D), BF16)
    wbcT = mk.sb("wbcT", (128, 8 * 2 * N), BF16)
    wg_ch = mk.sb("wg_chA", (128, 8 * ICW), BF16)
    wu_ch = mk.sb("wu_chA", (128, 8 * ICW), BF16)
    wd_ch = mk.sb("wd_chA", (128, (ICW // 128) * D), BF16)
    convw = mk.sb("convw", (128, DG * K4), F32)
    convb = mk.sb("convb", (128, DG), F32)
    bdt = mk.sb("bdt", (128, DG), F32)
    bin_sb = mk.sb("bin_sb", (128, 16), F32)
    bbc = mk.sb("bbc", (64, 1), F32)
    A_sb = mk.sb("A_sb", (128, DG * N), F32)
    g_rep = mk.sb("g_rep", (128, D), BF16)
    b_rep = mk.sb("b_rep", (128, D), BF16)
    ob_rep = mk.sb("ob_rep", (128, D), BF16)
    carry = mk.sb("carry", (128, DG * N * B), BF16)

    xn_tm = mk.sb("xn_tm", (128, B * TPB * D), F32)
    xn_fm = mk.sb("xn_fm", (128, DG * W3), BF16)
    xc = mk.sb("xc", (128, W3), F32)
    xs = mk.sb("xs", (128, DG * W), BF16)
    sz = mk.sb("sz", (128, DG * W), BF16)
    dt_sb = mk.sb("dt_sb", (128, DG * W), BF16)
    dtx = mk.sb("dtx", (128, DG * W), BF16)
    bc_fm = mk.sb("bc_fm", (64, W), BF16)
    a_bufs = [mk.sb(f"a_buf{i}", (128, WC4), F32) for i in range(2)]
    u_bufs = [mk.sb(f"u_buf{i}", (128, WC4), BF16) for i in range(2)]
    V_bufs = [mk.sb(f"V_buf{i}", (128, 4 * W), BF16) for i in range(2)]
    Breps = [mk.sb(f"Brep{i}", (128, W), BF16) for i in range(2)]
    Creps = [mk.sb(f"Crep{i}", (128, W), BF16) for i in range(2)]
    Yacc = mk.sb("Yacc", (128, 4 * W), F32)
    y_fm = sz
    xq_tm = mk.sb("xq_tm", (128, D), BF16)
    SCR_A = mk.sb("SCR_A", (128, D), F32)
    q_bm = mk.sb("q_bm", (128, 64), F32)
    q_sc = mk.sb("q_sc", (128, 64), F32)
    q_rs = mk.sb("q_rs", (128, 64), F32)
    q_v = mk.sb("q_v", (128, D), F32)
    q_p1 = mk.sb("q_p1", (128, D), F32)
    q_p2 = mk.sb("q_p2", (128, D), F32)
    q_p3 = mk.sb("q_p3", (128, D), F32)
    qscr = dict(bm=q_bm, sc=q_sc, rs=q_rs, v=q_v[:], p1=q_p1[:], p2=q_p2[:], p3=q_p3[:])
    st_sum = mk.sb("st_sum", (128, 1), F32)
    st_sq = mk.sb("st_sq", (128, 1), F32)
    st_m = mk.sb("st_m", (128, 1), F32)
    st_r = mk.sb("st_r", (128, 1), F32)
    lnscr = dict(st_sum=st_sum, st_sq=st_sq, st_m=st_m, st_r=st_r, sq=q_v[:])
    h_sb = mk.sb("h_sb", (128, 512), F32)
    hq_tm = mk.sb("hq_tm", (128, 512), BF16)
    hq_fm = mk.sb("hq_fm", (128, 4 * 128), BF16)
    dacc = mk.sb("dacc", (128, (W // 128) * D), BF16)
    attn_sb = mk.sb("attn_sb", (128, D), F32)
    xt = attn_sb
    xn1 = SCR_A[0:1, :D]
    xres = q_p2[:, :D]
    o_sb = mk.sb("o_sb", (128, D), F32)
    cacc = q_v[:, :W] if W <= D else mk.sb("cacc", (128, W), F32)
    sp1 = q_p1[:, :512]
    sp2 = q_p2[:, :512]
    sp3 = q_p3[:, :512]
    hs_sb = xc[:, :512] if W3 >= 512 else mk.sb("hs_sb", (128, 512), F32)
    WS = SCR_A
    WPREP = sz[:, :2048]
    WQS = WPREP[:, :1024]
    WTS = WPREP[:, 1024:2048]

    ps1 = mk.ps("ps1", (128, 512))
    ps2 = mk.ps("ps2", (128, 512))
    ps_g = mk.ps("ps_g", (128, 512))
    ps_y = mk.ps("ps_y", (128, 512))
    ps_o = mk.ps("ps_o", (128, 512))
    ps_g2 = mk.ps("ps_g2", (128, 512))
    ps_y2 = mk.ps("ps_y2", (128, 512))
    ps_tb = mk.ps("ps_tb", (128, 128), BF16)

    V, S, G, TMM, TTR = mk.v, mk.s, mk.g, mk.t_mm, mk.t_tr
    dma = p.dma

    dma("sync", ident_f[:], ident_in[:], "prep")
    V(lambda e: nc.vector.tensor_copy(ident_b[:], ident_f[:]), (ident_f,), (ident_b,), "prep")

    def cast_to_bounce(src, dst, ncols, tag):
        for kk in range(8):
            c0 = 0
            while c0 < ncols:
                cw = min(D, ncols - c0)
                stv = WS[:, :cw]
                dma("sync", stv, src[kk * 128:(kk + 1) * 128, c0:c0 + cw], tag)
                qv = WQS[:, :cw]
                V(lambda e, qv=qv, stv=stv: nc.vector.tensor_copy(qv, stv), (SCR_A,), (sz,), tag)
                dma("scalar", dst[kk * 128:(kk + 1) * 128, c0:c0 + cw], qv, tag)
                c0 += cw

    cast_to_bounce(w_inT, winT_d, 2 * D, "prep:win")
    cast_to_bounce(w_dtT, wdtT_d, D, "prep:wdt")
    cast_to_bounce(w_outT, woutT_d, D, "prep:wout")

    for kk in range(8):
        stv = WS[:, :2 * N]
        dma("sync", stv, w_bcT[kk * 128:(kk + 1) * 128, :], "prep:wbc")
        dstv = wbcT[:, kk * 2 * N:(kk + 1) * 2 * N]
        V(lambda e, dstv=dstv, stv=stv: nc.vector.tensor_copy(dstv, stv), (SCR_A,), (wbcT,), "prep")

    for dg in range(DG):
        dma("sync", convw[:, dg * K4:(dg + 1) * K4], conv_w[dg * 128:(dg + 1) * 128, :], "prep")
        dma("sync", convb[:, dg:dg + 1], conv_b[dg * 128:(dg + 1) * 128].unsqueeze(1), "prep")
        dma("sync", bdt[:, dg:dg + 1], b_dt[dg * 128:(dg + 1) * 128].unsqueeze(1), "prep")
        stv = WS[:, :N]
        dma("sync", stv, A_log[dg * 128:(dg + 1) * 128, :], "prep")
        a_d = A_sb[:, dg * N:(dg + 1) * N]
        V(lambda e, a_d=a_d, stv=stv: nc.vector.tensor_scalar(a_d, stv, -5.0, 2.0, OP.max, OP.min),
          (SCR_A,), (A_sb,), "prep")
        S(lambda e, a_d=a_d: nc.scalar.activation(a_d, a_d, AF.Exp), (A_sb,), (A_sb,), "prep")
        V(lambda e, a_d=a_d: nc.vector.tensor_scalar(a_d, a_d, -1.0, None, OP.mult),
          (A_sb,), (A_sb,), "prep")
    for jg in range(16):
        dma("sync", bin_sb[:, jg:jg + 1], b_in[jg * 128:(jg + 1) * 128].unsqueeze(1), "prep")
    dma("sync", bbc[:], b_bc[:].unsqueeze(1), "prep")
    for src_d, dst_sb in ((ln_g, g_rep), (ln_b, b_rep), (b_out, ob_rep)):
        stv = WS[:, :D]
        dma("sync", stv, src_d[:].unsqueeze(0).broadcast_to([128, D]), "prep")
        V(lambda e, dst_sb=dst_sb, stv=stv: nc.vector.tensor_copy(dst_sb[:], stv),
          (SCR_A,), (dst_sb,), "prep")
    G(lambda e: nc.gpsimd.memset(carry[:], 0.0), (),
      tuple([(carry, "carry%d" % (i * 4 * B)) for i in range(2 * N)]), "prep")
    G(lambda e: nc.gpsimd.memset(a_bufs[0][:], 0.0), (), (a_bufs[0],), "prep")
    G(lambda e: nc.gpsimd.memset(a_bufs[1][:], 0.0), (), (a_bufs[1],), "prep")

    def prep_ffn_weight(w_src, wq_dst, nrows, ncols, tag):
        for rg in range(nrows // 128):
            c0 = 0
            while c0 < ncols:
                cw = min(D, ncols - c0)
                stv = WS[:, :cw]
                dma("sync", stv, w_src[rg * 128:(rg + 1) * 128, c0:c0 + cw], tag)
                qd = WQS[:, :cw]
                mk.emit_quant(stv, qd, qscr, cw // 16, tag)
                for it in range(cw // 128):
                    TTR(ps_tb[:], qd[:, it * 128:(it + 1) * 128], ident_b[:], tag)
                    dtile = WTS[:, it * 128:(it + 1) * 128]
                    V(lambda e, dtile=dtile: nc.vector.tensor_copy(dtile, ps_tb[:]),
                      (ps_tb,), (sz,), tag)
                o = wq_dst[c0:c0 + cw, rg * 128:(rg + 1) * 128].rearrange(
                    "(it p) c -> p it c", p=128)
                i_ = WTS[:, :cw].rearrange("p (it c) -> p it c", c=128)
                dma("scalar", o, i_, tag)
                c0 += cw

    prep_ffn_weight(w_g, wq_gT_d, I, D, "prep:wg")
    prep_ffn_weight(w_u, wq_uT_d, I, D, "prep:wu")
    prep_ffn_weight(w_d, wq_dT_d, D, I, "prep:wd")

    NTG = W // 128

    for ph in range(NPH):
        t0 = ph * T
        tagp = f"p{ph}"

        if ph > 0:
            for dg in range(DG):
                w3v = xn_fm[:, dg * W3:(dg + 1) * W3].rearrange("p (b c) -> p b c", c=T + 3)
                src = w3v[:, :, T:T + 2]
                dst = w3v[:, :, 0:2]
                V(lambda e, dst=dst, src=src: nc.vector.tensor_copy(dst, src),
                  (xn_fm,), (xn_fm,), tagp + ":tail")
        else:
            G(lambda e: nc.gpsimd.memset(xn_fm[:], 0.0), (), (xn_fm,), tagp)

        for b in range(B):
            for j in range(TPB):
                ta = t0 + j * 128
                dma("sync", xt[:], x_in[b, ta:ta + 128, :], tagp + ":x")
                xnt = xn_tm[:, (b * TPB + j) * D:(b * TPB + j + 1) * D]
                mk.emit_ln(xt[:], xnt, lnscr, tag=tagp + ":ln")
                for dg in range(DG):
                    TTR(ps1[:, 0:128], xnt[:, dg * 128:(dg + 1) * 128], ident_f[:], tagp + ":tr")
                    c0 = dg * W3 + b * (T + 3) + 2 + j * 128
                    dcol = xn_fm[:, c0:c0 + 128]
                    V(lambda e, dcol=dcol: nc.vector.tensor_copy(dcol, ps1[:, 0:128]),
                      (ps1,), (xn_fm,), tagp + ":tre")
            if ph < NPH - 1:
                dma("sync", xt[0:1, :], x_in[b, t0 + T:t0 + T + 1, :], tagp + ":x1")
                mk.emit_ln(xt[0:1, :], xn1[:], lnscr, tag=tagp + ":ln1")
                for dg in range(DG):
                    TTR(ps1[0:128, 0:1], xn1[:, dg * 128:(dg + 1) * 128], ident_f[0:1, 0:1], tagp + ":tr1")
                    c0 = dg * W3 + b * (T + 3) + 2 + T
                    dcol = xn_fm[:, c0:c0 + 1]
                    V(lambda e, dcol=dcol: nc.vector.tensor_copy(dcol, ps1[:, 0:1]),
                      (ps1,), (xn_fm,), tagp + ":tre1")

        for jg in range(16):
            is_xc = jg < 8
            dg = jg % 8
            wch = win_ch[jg % 2]
            wsrc = winT_d[:, jg * 128:(jg + 1) * 128].rearrange("(kk p) c -> p kk c", p=128)
            dma("sync", wch[:].rearrange("p (kk c) -> p kk c", c=128), wsrc, tagp + ":winl")
            bias = bin_sb[:, jg:jg + 1]
            if is_xc:
                for pr in range(2):
                    pp = (ps1, ps2)[pr]
                    ncol = 2 * (T + 3)
                    for kk in range(8):
                        lhs = wch[:, kk * 128:(kk + 1) * 128]
                        rhs = xn_fm[:, kk * W3 + pr * ncol:kk * W3 + (pr + 1) * ncol]
                        TMM(pp[:, 0:ncol], lhs, rhs, start=(kk == 0), stop=(kk == 7), tag=tagp + ":inp")
                    dst = xc[:, pr * ncol:(pr + 1) * ncol]
                    V(lambda e, dst=dst, pp=pp, ncol=ncol, bias=bias:
                      nc.vector.tensor_scalar(dst, pp[:, 0:ncol], bias, None, OP.add),
                      (pp, bin_sb, wch), (xc,), tagp + ":xce")
                if ph == 0:
                    for b in range(B):
                        pad = xc[:, b * (T + 3):b * (T + 3) + 2]
                        V(lambda e, pad=pad: nc.vector.memset(pad, 0.0), (), (xc,), tagp)
                if ph == NPH - 1:
                    for b in range(B):
                        pad = xc[:, b * (T + 3) + T + 2:b * (T + 3) + T + 3]
                        V(lambda e, pad=pad: nc.vector.memset(pad, 0.0), (), (xc,), tagp)
                cv = cacc[:].rearrange("p (b c) -> p b c", c=T)
                wv = convw[:, dg * K4:dg * K4 + 1]
                x_sh = xc[:].rearrange("p (b c) -> p b c", c=T + 3)[:, :, 0:T]
                V(lambda e, cv=cv, x_sh=x_sh, wv=wv:
                  nc.vector.tensor_scalar(cv, x_sh, wv, None, OP.mult),
                  (xc, convw), (cacc,), tagp + ":cv0")
                for k in range(1, K4):
                    wv = convw[:, dg * K4 + k:dg * K4 + k + 1]
                    x_sh = xc[:].rearrange("p (b c) -> p b c", c=T + 3)[:, :, k:k + T]
                    V(lambda e, cv=cv, x_sh=x_sh, wv=wv:
                      nc.vector.scalar_tensor_tensor(cv, x_sh, wv, cv, OP.mult, OP.add),
                      (xc, convw, cacc), (cacc,), tagp + f":cv{k}")
                xs_d = xs[:, dg * W:(dg + 1) * W]
                cb = convb[:, dg:dg + 1]
                S(lambda e, xs_d=xs_d, cb=cb:
                  nc.scalar.activation(xs_d, cacc[:], AF.Silu, bias=cb),
                  (cacc, convb), (xs,), tagp + ":xs")
            else:
                pp = (ps1, ps2)[jg % 2]
                ppv = pp[:, 0:W].rearrange("p (b t) -> p b t", t=T)
                for kk in range(8):
                    lhs = wch[:, kk * 128:(kk + 1) * 128]
                    rhs = xn_fm[:, kk * W3:(kk + 1) * W3].rearrange(
                        "p (b c) -> p b c", c=T + 3)[:, :, 2:2 + T]
                    TMM(ppv, lhs, rhs, start=(kk == 0), stop=(kk == 7), tag=tagp + ":zmm")
                dst = sz[:, dg * W:(dg + 1) * W]
                S(lambda e, dst=dst, pp=pp, bias=bias:
                  nc.scalar.activation(dst, pp[:, 0:W], AF.Silu, bias=bias),
                  (pp, bin_sb, wch), (sz,), tagp + ":z")

        for dgo in range(DG):
            wch = wdt_ch[dgo % 2]
            wsrc = wdtT_d[:, dgo * 128:(dgo + 1) * 128].rearrange("(kk p) c -> p kk c", p=128)
            dma("sync", wch[:].rearrange("p (kk c) -> p kk c", c=128), wsrc, tagp + ":wdtl")
            pp = (ps1, ps2)[dgo % 2]
            for kk in range(8):
                lhs = wch[:, kk * 128:(kk + 1) * 128]
                rhs = xs[:, kk * W:kk * W + W]
                TMM(pp[:, 0:W], lhs, rhs, start=(kk == 0), stop=(kk == 7), tag=tagp + ":dtm")
            bias = bdt[:, dgo:dgo + 1]
            V(lambda e, pp=pp, bias=bias:
              nc.vector.tensor_scalar(sp1[:, :W], pp[:, :W], bias, 0.0, OP.add, OP.max),
              (pp, bdt), (sp1,), tagp)
            V(lambda e, pp=pp, bias=bias:
              nc.vector.tensor_scalar(sp3[:, :W], pp[:, :W], bias, None, OP.add),
              (pp, bdt), (sp3,), tagp)
            V(lambda e: nc.vector.scalar_tensor_tensor(sp2[:, :W], sp1[:, :W], -2.0, sp3[:, :W], OP.mult, OP.add),
              (sp1, sp3), (sp2,), tagp)
            S(lambda e: nc.scalar.activation(sp2[:, :W], sp2[:, :W], AF.Exp), (sp2,), (sp2,), tagp)
            S(lambda e: nc.scalar.activation(sp2[:, :W], sp2[:, :W], AF.Ln, bias=1.0), (sp2,), (sp2,), tagp)
            V(lambda e: nc.vector.tensor_tensor(sp1[:, :W], sp1[:, :W], sp2[:, :W], OP.add),
              (sp1, sp2), (sp1,), tagp)
            dtd = dt_sb[:, dgo * W:(dgo + 1) * W]
            V(lambda e, dtd=dtd: nc.vector.tensor_scalar(dtd, sp1[:, :W], 0.2, 0.001, OP.min, OP.max),
              (sp1, wch), (dt_sb,), tagp + ":dt")
            xsd = xs[:, dgo * W:(dgo + 1) * W]
            dxd = dtx[:, dgo * W:(dgo + 1) * W]
            V(lambda e, dxd=dxd, dtd=dtd, xsd=xsd: nc.vector.tensor_tensor(dxd, dtd, xsd, OP.mult),
              (dt_sb, xs), (dtx,), tagp + ":dtx")
        pp = ps1
        for kk in range(8):
            lhs = wbcT[:, kk * 2 * N:(kk + 1) * 2 * N]
            rhs = xs[:, kk * W:kk * W + W]
            TMM(pp[0:64, 0:W], lhs, rhs, start=(kk == 0), stop=(kk == 7), tag=tagp + ":bcm")
        V(lambda e: nc.vector.tensor_scalar(bc_fm[:, :W], pp[0:64, :W], bbc[:], None, OP.add),
          (pp, bbc), (bc_fm,), tagp + ":bce")
        dma("scalar", bc_bounce[:], bc_fm[:], tagp + ":bcb")

        for tg in range(NTG):
            xnt = xn_tm[:, tg * D:(tg + 1) * D]
            mk.emit_quant(xnt, xq_tm[:], qscr, D // 16, tagp + ":xq")
            for dg in range(DG):
                TTR(ps_tb[:], xq_tm[:, dg * 128:(dg + 1) * 128], ident_b[:], tagp + ":qtr")
                dcol = xs[:, dg * W + tg * 128:dg * W + (tg + 1) * 128]
                V(lambda e, dcol=dcol: nc.vector.tensor_copy(dcol, ps_tb[:]),
                  (ps_tb,), (xs,), tagp + ":qtre")
        xq_fm = xs

        seg = lambda buf: buf[:].rearrange("p (g b c) -> p g b c", b=B, c=WCB)

        def emit_uc(half, n):
            pi = n % 2
            u_buf = u_bufs[pi]
            u_C = seg(u_buf)[:, :, :, 0:1]
            cidx = (n * 2 + half) * 4 * B
            cv4 = carry[:, cidx:cidx + 4 * B].rearrange("p (g b o) -> p g b o", b=B, o=1)
            G(lambda e, cv4=cv4, u_C=u_C: nc.gpsimd.tensor_copy(u_C, cv4),
              ((carry, "carry%d" % cidx),), ((u_bufs[pi], "uc%d" % pi),), tagp + ":uc")

        def d_pre(half):
            V(lambda e: nc.vector.memset(Yacc[:], 0.0), (), (Yacc,), tagp + ":ya0")
            emit_uc(half, 0)
            emit_uc(half, 1)

        def d_n(half, n):
            pi = n % 2
            a_buf, V_buf, Brep, Crep = a_bufs[pi], V_bufs[pi], Breps[pi], Creps[pi]
            u_buf = u_bufs[pi]
            ukm = ("um%d" % pi)
            ukc = ("uc%d" % pi)
            a_T = seg(a_buf)[:, :, :, 1:WCB]
            u_T = seg(u_buf)[:, :, :, 1:WCB]
            s_T = u_T
            s_L = seg(u_buf)[:, :, :, T:T + 1]
            cidx = (n * 2 + half) * 4 * B
            cv4 = carry[:, cidx:cidx + 4 * B].rearrange("p (g b o) -> p g b o", b=B, o=1)
            dma("sync", Brep[:], bc_bounce[n:n + 1, :].broadcast_to([128, W]), tagp + ":Br")
            dma("sync", Crep[:], bc_bounce[N + n:N + n + 1, :].broadcast_to([128, W]), tagp + ":Cr")
            for dgi in range(4):
                dg = half * 4 + dgi
                dtd = dt_sb[:, dg * W:(dg + 1) * W].rearrange("p (b t) -> p b t", t=T)
                aT1 = a_T[:, dgi]
                S(lambda e, dtd=dtd, dg=dg, n=n, aT1=aT1:
                  nc.scalar.activation(aT1, dtd, AF.Exp, scale=A_sb[:, dg * N + n:dg * N + n + 1]),
                  (dt_sb, A_sb), (a_buf,), tagp + ":a")
            dxh = dtx[:, half * 4 * W:(half * 4 + 4) * W].rearrange("p (g b t) -> p g b t", b=B, t=T)
            br4 = Brep[:].rearrange("p (b t) -> p b t", t=T).unsqueeze(1).broadcast_to([128, 4, B, T])
            V(lambda e, dxh=dxh, br4=br4, u_T=u_T: nc.vector.tensor_tensor(u_T, dxh, br4, OP.mult),
              (dtx, Brep), ((u_buf, ukm),), tagp + ":u")
            V(lambda e, a_buf=a_buf, u_buf=u_buf: nc.vector.tensor_tensor_scan(u_buf[:], a_buf[:], u_buf[:], 0.0, OP.mult, OP.add),
              (a_buf, (u_buf, ukm), (u_buf, ukc)), ((u_buf, ukm), (u_buf, ukc)), tagp + ":tts")
            G(lambda e, cv4=cv4, s_L=s_L: nc.gpsimd.tensor_copy(cv4, s_L),
              (((u_buf, ukm)),), ((carry, "carry%d" % cidx),), tagp + ":cs")
            if n + 2 < N:
                emit_uc(half, n + 2)
            cr4 = Crep[:].rearrange("p (b t) -> p b t", t=T).unsqueeze(1).broadcast_to([128, 4, B, T])
            Vv = V_buf[:].rearrange("p (g b t) -> p g b t", b=B, t=T)
            V(lambda e, cr4=cr4, Vv=Vv, s_T=s_T: nc.vector.tensor_tensor(Vv, s_T, cr4, OP.mult),
              (((u_buf, ukm)), Crep), (V_buf,), tagp + ":V")
            G(lambda e, V_buf=V_buf: nc.gpsimd.tensor_tensor(Yacc[:], Yacc[:], V_buf[:], OP.add),
              (Yacc, V_buf), (Yacc,), tagp + ":ya")

        def d_post(half):
            for dgi in range(4):
                dg = half * 4 + dgi
                yv = Yacc[:, dgi * W:(dgi + 1) * W]
                szd = sz[:, dg * W:(dg + 1) * W]
                V(lambda e, yv=yv, szd=szd: nc.vector.tensor_tensor(szd, yv, szd, OP.mult),
                  (Yacc, sz), (sz,), tagp + ":y")

        d_units = []
        for half in range(2):
            d_units.append(lambda half=half: d_pre(half))
            for n in range(N):
                d_units.append(lambda half=half, n=n: d_n(half, n))
            d_units.append(lambda half=half: d_post(half))

        def f_load(ci):
            ic0, icw = CHUNKS[ci]
            gsrc = wq_gT_d[:, ic0:ic0 + icw].rearrange("(kk p) c -> p kk c", p=128)
            dma("scalar", wg_ch[:].rearrange("p (kk c) -> p kk c", c=ICW)[:, :, :icw], gsrc, tagp + ":wgl")
            usrc = wq_uT_d[:, ic0:ic0 + icw].rearrange("(kk p) c -> p kk c", p=128)
            dma("scalar", wu_ch[:].rearrange("p (kk c) -> p kk c", c=ICW)[:, :, :icw], usrc, tagp + ":wul")
            dsrc = wq_dT_d[ic0:ic0 + icw, :].rearrange("(it p) c -> p it c", p=128)
            dma("scalar", wd_ch[:, :(icw // 128) * D].rearrange("p (it c) -> p it c", c=D), dsrc, tagp + ":wdl")

        def f_dacc0():
            G(lambda e: nc.gpsimd.memset(dacc[:], 0.0), (), (dacc,), tagp + ":da0")

        def f_mm(ci, tg):
            u = (ci * NTG + tg) % 2
            pg, py = (ps_g, ps_y) if u == 0 else (ps_g2, ps_y2)
            for kk in range(8):
                lhs = xq_fm[:, kk * W + tg * 128:kk * W + (tg + 1) * 128]
                TMM(pg[:, :ICW], lhs, wg_ch[:, kk * ICW:(kk + 1) * ICW],
                    start=(kk == 0), stop=(kk == 7), tag=tagp + ":gm")
            for kk in range(8):
                lhs = xq_fm[:, kk * W + tg * 128:kk * W + (tg + 1) * 128]
                TMM(py[:, :ICW], lhs, wu_ch[:, kk * ICW:(kk + 1) * ICW],
                    start=(kk == 0), stop=(kk == 7), tag=tagp + ":um")

        def f_post(ci, tg):
            ic0, icw = CHUNKS[ci]
            nit = icw // 128
            u = (ci * NTG + tg) % 2
            pg, py = (ps_g, ps_y) if u == 0 else (ps_g2, ps_y2)
            S(lambda e, icw=icw, pg=pg: nc.scalar.activation(hs_sb[:, :icw], pg[:, :icw], AF.Silu),
              (pg,), (hs_sb,), tagp + ":hs")
            V(lambda e, icw=icw, py=py: nc.vector.tensor_tensor(h_sb[:, :icw], hs_sb[:, :icw], py[:, :icw], OP.mult),
              (hs_sb, py, wg_ch, wu_ch), (h_sb,), tagp + ":h")
            mk.emit_quant(h_sb[:, :icw], hq_tm[:, :icw], qscr, icw // 16, tagp + ":hq")
            for it in range(nit):
                TTR(ps_tb[:], hq_tm[:, it * 128:(it + 1) * 128], ident_b[:], tagp + ":htr")
                dcol = hq_fm[:, it * 128:(it + 1) * 128]
                V(lambda e, dcol=dcol: nc.vector.tensor_copy(dcol, ps_tb[:]),
                  (ps_tb,), (hq_fm,), tagp + ":htre")
            for jc in range(2):
                for it in range(nit):
                    TMM(ps_o[:, 0:512], hq_fm[:, it * 128:(it + 1) * 128],
                        wd_ch[:, it * D + jc * 512:it * D + (jc + 1) * 512],
                        start=(it == 0), stop=(it == nit - 1), tag=tagp + ":dm")
                dv = dacc[:, tg * D + jc * 512:tg * D + (jc + 1) * 512]
                V(lambda e, dv=dv: nc.vector.tensor_tensor(dv, dv, ps_o[:, 0:512], OP.add),
                  (dacc, ps_o, wd_ch), (dacc,), tagp + ":dacc")

        def f_woutl():
            wsrc = woutT_d[:, :].rearrange("(kk p) c -> p kk c", p=128)
            dma("scalar", wout_ph[:].rearrange("p (kk c) -> p kk c", c=D), wsrc, tagp + ":wol")

        f_units = [lambda: f_load(0), f_dacc0, f_woutl]
        for ci in range(NCH):
            pend = None
            for tg in range(NTG):
                f_units.append(lambda ci=ci, tg=tg: f_mm(ci, tg))
                if pend is not None:
                    f_units.append(lambda p=pend: f_post(*p))
                pend = (ci, tg)
            f_units.append(lambda p=pend: f_post(*p))
            if ci + 1 < NCH:
                f_units.append(lambda ci=ci: f_load(ci + 1))

        INTERLEAVE = True
        if INTERLEAVE:
            npre = 3
            for k in range(npre):
                f_units[k]()
            f_done = npre
            nf, nd = len(f_units), len(d_units)
            for di, du in enumerate(d_units):
                du()
                want = npre + (di + 1) * (nf - npre) // nd
                while f_done < want:
                    f_units[f_done]()
                    f_done += 1
            while f_done < nf:
                f_units[f_done]()
                f_done += 1
        else:
            for du in d_units:
                du()
            for fu in f_units:
                fu()

        for tg in range(NTG):
            b, j = tg // TPB, tg % TPB
            for jc in range(2):
                for kk in range(DG):
                    lhs = y_fm[:, kk * W + tg * 128:kk * W + (tg + 1) * 128]
                    TMM(ps_o[:, 0:512], lhs,
                        wout_ph[:, kk * D + jc * 512:kk * D + (jc + 1) * 512],
                        start=(kk == 0), stop=(kk == 7), tag=tagp + ":om")
                ob_h = ob_rep[:, jc * 512:(jc + 1) * 512]
                at_h = attn_sb[:, jc * 512:(jc + 1) * 512]
                V(lambda e, ob_h=ob_h, at_h=at_h: nc.vector.tensor_tensor(at_h, ps_o[:, 0:512], ob_h, OP.add),
                  (ps_o, ob_rep, wout_ph), (attn_sb,), tagp + ":ob")
            xnt = xn_tm[:, tg * D:(tg + 1) * D]
            V(lambda e, xnt=xnt: nc.vector.tensor_tensor(attn_sb[:], attn_sb[:], xnt, OP.add),
              (attn_sb, xn_tm), (attn_sb,), tagp + ":resm")
            mk.emit_ln(attn_sb[:], attn_sb[:], lnscr, gamma=g_rep[:], beta=b_rep[:], tag=tagp + ":mln")
            ta = t0 + j * 128
            dma("sync", xres[:], x_in[b, ta:ta + 128, :], tagp + ":xr")
            V(lambda e: nc.vector.tensor_tensor(o_sb[:], attn_sb[:], xres[:], OP.add),
              (attn_sb, q_p2), (o_sb,), tagp + ":fin1")
            dv = dacc[:, tg * D:(tg + 1) * D]
            V(lambda e, dv=dv: nc.vector.tensor_tensor(o_sb[:], o_sb[:], dv, OP.add),
              (o_sb, dacc), (o_sb,), tagp + ":fin2")
            dma("scalar", out_t[b, ta:ta + 128, :], o_sb[:], tagp + ":out")

    V(lambda e: nc.vector.engine_nop(), (out_t,), (), "final")

    from contextlib import ExitStack
    with ExitStack() as es:
        _sem_n = [0]

        def sem_factory(key):
            _sem_n[0] += 1
            return es.enter_context(nc.semaphore(f"sm{_sem_n[0]}"))

        block = es.enter_context(nc.Block())
        p.build(block, sem_factory)
    nc.finalize()
    return nc


_NC_CACHE = {}


def _get_nc(B, L, T):
    key = (B, L, T)
    if key not in _NC_CACHE:
        _NC_CACHE[key] = build_core_program(B, L, T)
    return _NC_CACHE[key]


def make_in_map(inputs, xshard):
    c = lambda a: np.ascontiguousarray(np.asarray(a), dtype=np.float32)
    return dict(
        x=c(xshard),
        w_inT=c(np.asarray(inputs["L0_attn_in_proj_w"]).T),
        b_in=c(inputs["L0_attn_in_proj_b"]),
        conv_w=c(np.asarray(inputs["L0_attn_conv_w"]).reshape(D, K4)),
        conv_b=c(inputs["L0_attn_conv_b"]),
        A_log=c(inputs["L0_attn_A_log"]),
        w_bcT=c(np.concatenate([np.asarray(inputs["L0_attn_B_w"]).T,
                                np.asarray(inputs["L0_attn_C_w"]).T], axis=1)),
        b_bc=c(np.concatenate([np.asarray(inputs["L0_attn_B_b"]),
                               np.asarray(inputs["L0_attn_C_b"])])),
        w_dtT=c(np.asarray(inputs["L0_attn_dt_w"]).T),
        b_dt=c(inputs["L0_attn_dt_b"]),
        w_outT=c(np.asarray(inputs["L0_attn_out_w"]).T),
        b_out=c(inputs["L0_attn_out_b"]),
        ln_g=c(inputs["L0_attn_ln_g"]),
        ln_b=c(inputs["L0_attn_ln_beta"]),
        w_g=c(inputs["L0_ffn_w_gate"]),
        w_u=c(inputs["L0_ffn_w_up"]),
        w_d=c(inputs["L0_ffn_w_down"]),
        ident=np.eye(128, dtype=np.float32),
    )


def kernel(**inputs):
    x = np.asarray(inputs["L0_attn_x"])
    NB = x.shape[0]
    ncores = 8
    bpc = NB // ncores
    nc = _get_nc(bpc, x.shape[1], 128)
    in_maps = [make_in_map(inputs, x[i * bpc:(i + 1) * bpc]) for i in range(ncores)]
    res = run_bass_kernel_spmd(nc, in_maps, core_ids=list(range(ncores)))
    out = np.concatenate([np.asarray(r["out"]) for r in res.results], axis=0)
    return out.astype(np.float32)


# version 16
# speedup vs baseline: 1.0385x; 1.0028x over previous
import numpy as np
import concourse.bass as bass
import concourse.mybir as mybir
from concourse.bass_utils import run_bass_kernel_spmd

F32 = mybir.dt.float32
BF16 = mybir.dt.bfloat16
AF = mybir.ActivationFunctionType
OP = mybir.AluOpType
AX = mybir.AxisListType

D = 1024
N = 32
DG = 8
I = 2816
K4 = 4
EPS = 1e-5


class Prog:

    def __init__(self, nc):
        self.nc = nc
        self.ops = []
        self.last_write = {}
        self.reads_since = {}

    @staticmethod
    def _key(buf):
        if isinstance(buf, tuple):
            return buf[1]
        t = getattr(buf, "tensor", buf)
        return t.name if hasattr(t, "name") else str(t)

    def op(self, engine, emit, reads=(), writes=(), dma=False, tag=""):
        idx = len(self.ops)
        deps = set()
        for b in reads:
            w = self.last_write.get(self._key(b))
            if w is not None:
                deps.add(w)
        for b in writes:
            k = self._key(b)
            w = self.last_write.get(k)
            if w is not None:
                if not (dma and self.ops[w]["dma"]):
                    deps.add(w)
            for r in self.reads_since.get(k, ()):
                deps.add(r)
        self.ops.append(dict(engine=engine, emit=emit, deps=sorted(deps), dma=dma,
                             tag=tag, milestone=False, mcount=None, sem=None))
        for b in reads:
            self.reads_since.setdefault(self._key(b), []).append(idx)
        for b in writes:
            k = self._key(b)
            self.last_write[k] = idx
            self.reads_since[k] = []
        return idx

    def dma(self, queue, out, in_, tag=""):
        def emit(eng):
            return eng.dma_start(out, in_)
        idx = self.op(queue, emit, reads=(in_,), writes=(out,), dma=True, tag=tag)
        self.ops[idx]["dmakey"] = self._key(out)
        return idx

    def build(self, block, sem_factory):
        ops = self.ops
        for o in ops:
            for d in o["deps"]:
                ops[d]["milestone"] = True
        counts = {}
        for o in ops:
            if o["dma"]:
                key = ("dmab", o["dmakey"])
                counts[key] = counts.get(key, 0) + 1
                o["mcount"] = counts[key] * 16
                o["sem"] = key
                o["milestone"] = True
            elif o["milestone"]:
                key = o["engine"]
                counts[key] = counts.get(key, 0) + 1
                o["mcount"] = counts[key]
                o["sem"] = key
        sems = {}
        for o in ops:
            if o.get("sem") is not None and o["sem"] not in sems:
                sems[o["sem"]] = sem_factory(o["sem"])
        per_engine = {}
        for i, o in enumerate(ops):
            per_engine.setdefault(o["engine"], []).append(i)

        def emit_stream(eng_name, eng):
            waited = {}
            for i in per_engine.get(eng_name, []):
                o = ops[i]
                need = {}
                for d in o["deps"]:
                    dep = ops[d]
                    sk = dep["sem"]
                    need[sk] = max(need.get(sk, 0), dep["mcount"])
                for sk, val in sorted(need.items(), key=lambda kv: str(kv[0])):
                    if waited.get(sk, 0) < val:
                        eng.wait_ge(sems[sk], val)
                        waited[sk] = val
                inst = o["emit"](eng)
                if o["dma"]:
                    inst.then_inc(sems[o["sem"]], 16)
                elif o["milestone"]:
                    inst.then_inc(sems[o["sem"]], 1)

        @block.sync
        def _(eng):
            emit_stream("sync", eng)

        @block.tensor
        def _(eng):
            emit_stream("tensor", eng)

        @block.vector
        def _(eng):
            emit_stream("vector", eng)

        @block.scalar
        def _(eng):
            emit_stream("scalar", eng)

        @block.gpsimd
        def _(eng):
            emit_stream("gpsimd", eng)


class MK:
    def __init__(self, B=4, L=2048, T=128):
        assert L % T == 0 and T % 128 == 0
        self.B, self.L, self.T = B, L, T
        self.NPH = L // T
        self.TPB = T // 128
        self.W = B * T
        self.W3 = B * (T + 3)
        self.nc = bass.Bass("TRN2", target_bir_lowering=False, debug=False)
        self.p = Prog(self.nc)

    def sb(self, name, shape, dtype):
        return self.nc.alloc_sbuf_tensor(name, list(shape), dtype).ap()

    def ps(self, name, shape, dtype=F32):
        return self.nc.alloc_psum_tensor(name, list(shape), dtype).ap()

    def dram(self, name, shape, dtype, kind="Internal"):
        return self.nc.dram_tensor(name, list(shape), dtype, kind=kind).ap()

    def t_mm(self, out, lhsT, rhs, start, stop, tag="mm"):
        self.p.op("tensor",
                  lambda e: self.nc.tensor.matmul(out, lhsT, rhs, start=start, stop=stop),
                  reads=(lhsT, rhs), writes=(out,), tag=tag)

    def t_tr(self, out, in_, ident, tag="tr"):
        self.p.op("tensor", lambda e: self.nc.tensor.transpose(out, in_, ident),
                  reads=(in_, ident), writes=(out,), tag=tag)

    def v(self, fn, reads, writes, tag="v"):
        self.p.op("vector", fn, reads=reads, writes=writes, tag=tag)

    def s(self, fn, reads, writes, tag="s"):
        self.p.op("scalar", fn, reads=reads, writes=writes, tag=tag)

    def g(self, fn, reads, writes, tag="g"):
        self.p.op("gpsimd", fn, reads=reads, writes=writes, tag=tag)

    def emit_quant(self, x_ap, out_ap, scr, nblk, tag="q"):
        nc = self.nc
        P = x_ap.shape[0]
        F = nblk * 16
        bm, sc, rs = scr["bm"][:, :nblk], scr["sc"][:, :nblk], scr["rs"][:, :nblk]
        ax = scr["p1"][:, :F]
        av = scr["p2"][:, :F]
        lA = scr["p3"][:, :F]
        lB = scr["v"][:, :F]
        kb, kv, k1, k2, k3 = scr["bm"], scr["v"], scr["p1"], scr["p2"], scr["p3"]
        ksc, krs = scr["sc"], scr["rs"]
        V, S, G = self.v, self.s, self.g
        S(lambda e: nc.scalar.activation(ax, x_ap, AF.Abs), (x_ap,), (k1,), tag + ":ax")
        bm3 = ax.rearrange("p (b k) -> p b k", k=16)
        V(lambda e: nc.vector.reduce_max(bm, bm3, axis=AX.X), (k1,), (kb,), tag + ":bm")
        V(lambda e: nc.vector.tensor_scalar(sc, bm, float(1.0 / 6.0), 1e-12, OP.mult, OP.max),
          (kb,), (ksc,), tag)
        V(lambda e: nc.vector.reciprocal(rs, sc), (ksc,), (krs,), tag)
        rs_b = rs.unsqueeze(2).broadcast_to([P, nblk, 16])
        ax3 = ax.rearrange("p (b k) -> p b k", k=16)
        av3 = av.rearrange("p (b k) -> p b k", k=16)
        V(lambda e: nc.vector.tensor_tensor(av3, ax3, rs_b, OP.mult), (k1, krs), (k2,), tag)
        V(lambda e: nc.vector.tensor_scalar(lA, av, 0.25, None, OP.is_gt), (k2,), (k3,), tag)
        for th in (0.75, 1.25, 1.75):
            V(lambda e, th=th: nc.vector.scalar_tensor_tensor(lA, av, th, lA, OP.is_gt, OP.add),
              (k2, k3), (k3,), tag)
        V(lambda e: nc.vector.tensor_scalar(lB, av, 2.5, None, OP.is_gt), (k2,), (kv,), tag)
        V(lambda e: nc.vector.scalar_tensor_tensor(lB, av, 3.5, lB, OP.is_gt, OP.add),
          (k2, kv), (kv,), tag)
        V(lambda e: nc.vector.tensor_scalar(av, av, 5.0, 2.0, OP.is_gt, OP.mult), (k2,), (k2,), tag)
        S(lambda e: nc.scalar.activation(ax, x_ap, AF.Sign), (x_ap,), (k1,), tag + ":sg")
        V(lambda e: nc.vector.scalar_tensor_tensor(lA, lA, 0.5, lB, OP.mult, OP.add),
          (k3, kv), (k3,), tag)
        V(lambda e: nc.vector.tensor_tensor(lA, lA, av, OP.add), (k3, k2), (k3,), tag)
        G(lambda e: nc.gpsimd.tensor_tensor(lB, lA, ax, OP.mult), (k3, k1), (kv,), tag)
        sc_b = sc.unsqueeze(2).broadcast_to([P, nblk, 16])
        q3 = lB.rearrange("p (b k) -> p b k", k=16)
        o3 = out_ap.rearrange("p (b k) -> p b k", k=16)
        V(lambda e: nc.vector.tensor_tensor(o3, q3, sc_b, OP.mult),
          (kv, ksc), (out_ap,), tag + ":dq")

    def emit_ln(self, x_ap, out_ap, scr, gamma=None, beta=None, tag="ln"):
        nc = self.nc
        P = x_ap.shape[0]
        F = x_ap.shape[-1]
        ssum, ssq = scr["st_sum"][:P], scr["st_sq"][:P]
        sm, sr = scr["st_m"][:P], scr["st_r"][:P]
        sq = scr["sq"][:P, :F]
        ks = scr
        self.v(lambda e: nc.vector.reduce_sum(ssum, x_ap, axis=AX.X), (x_ap,), (ks["st_sum"],), tag)
        self.s(lambda e: nc.scalar.activation(sq, x_ap, AF.Square, accum_out=ssq),
               (x_ap,), (ks["sq"], ks["st_sq"]), tag)
        self.v(lambda e: nc.vector.tensor_scalar(sm, ssum, 1.0 / F, None, OP.mult),
               (ks["st_sum"],), (ks["st_m"],), tag)
        self.v(lambda e: nc.vector.tensor_scalar(sr, ssq, 1.0 / F, EPS, OP.mult, OP.add),
               (ks["st_sq"],), (ks["st_r"],), tag)
        self.v(lambda e: nc.vector.scalar_tensor_tensor(ssq, sm, -1.0, sm, OP.mult, OP.mult),
               (ks["st_m"],), (ks["st_sq"],), tag)
        self.v(lambda e: nc.vector.tensor_tensor(sr, sr, ssq, OP.add),
               (ks["st_r"], ks["st_sq"]), (ks["st_r"],), tag)
        self.s(lambda e: nc.scalar.activation(sr, sr, AF.Sqrt), (ks["st_r"],), (ks["st_r"],), tag)
        self.v(lambda e: nc.vector.reciprocal(sr, sr), (ks["st_r"],), (ks["st_r"],), tag)
        self.v(lambda e: nc.vector.scalar_tensor_tensor(sm, sm, -1.0, sr, OP.mult, OP.mult),
               (ks["st_m"], ks["st_r"]), (ks["st_m"],), tag)
        self.v(lambda e: nc.vector.tensor_scalar(out_ap, x_ap, sr, sm, OP.mult, OP.add),
               (x_ap, ks["st_r"], ks["st_m"]), (out_ap,), tag + ":nrm")
        if gamma is not None:
            self.v(lambda e: nc.vector.tensor_tensor(out_ap, out_ap, gamma, OP.mult),
                   (out_ap, gamma), (out_ap,), tag)
            self.v(lambda e: nc.vector.tensor_tensor(out_ap, out_ap, beta, OP.add),
                   (out_ap, beta), (out_ap,), tag)


def build_core_program(B=4, L=2048, T=128):
    mk = MK(B, L, T)
    nc, p = mk.nc, mk.p
    W, W3, TPB, NPH = mk.W, mk.W3, mk.TPB, mk.NPH
    WCB = 1 + T
    WC4 = 4 * B * WCB
    ICW = 512
    CHUNKS = []
    _c0 = 0
    while _c0 < I:
        CHUNKS.append((_c0, min(ICW, I - _c0)))
        _c0 += ICW
    NCH = len(CHUNKS)

    x_in = mk.dram("x", (B, L, D), F32, kind="ExternalInput")
    out_t = mk.dram("out", (B, L, D), F32, kind="ExternalOutput")
    w_inT = mk.dram("w_inT", (D, 2 * D), F32, kind="ExternalInput")
    b_in = mk.dram("b_in", (2 * D,), F32, kind="ExternalInput")
    conv_w = mk.dram("conv_w", (D, K4), F32, kind="ExternalInput")
    conv_b = mk.dram("conv_b", (D,), F32, kind="ExternalInput")
    A_log = mk.dram("A_log", (D, N), F32, kind="ExternalInput")
    w_bcT = mk.dram("w_bcT", (D, 2 * N), F32, kind="ExternalInput")
    b_bc = mk.dram("b_bc", (2 * N,), F32, kind="ExternalInput")
    w_dtT = mk.dram("w_dtT", (D, D), F32, kind="ExternalInput")
    b_dt = mk.dram("b_dt", (D,), F32, kind="ExternalInput")
    w_outT = mk.dram("w_outT", (D, D), F32, kind="ExternalInput")
    b_out = mk.dram("b_out", (D,), F32, kind="ExternalInput")
    ln_g = mk.dram("ln_g", (D,), F32, kind="ExternalInput")
    ln_b = mk.dram("ln_b", (D,), F32, kind="ExternalInput")
    w_g = mk.dram("w_g", (I, D), F32, kind="ExternalInput")
    w_u = mk.dram("w_u", (I, D), F32, kind="ExternalInput")
    w_d = mk.dram("w_d", (D, I), F32, kind="ExternalInput")
    ident_in = mk.dram("ident", (128, 128), F32, kind="ExternalInput")

    bc_bounce = mk.dram("bc_bounce", (2 * N, W), BF16)
    winT_d = mk.dram("winT_d", (D, 2 * D), BF16)
    wdtT_d = mk.dram("wdtT_d", (D, D), BF16)
    woutT_d = mk.dram("woutT_d", (D, D), BF16)
    wq_gT_d = mk.dram("wq_gT_d", (D, I), BF16)
    wq_uT_d = mk.dram("wq_uT_d", (D, I), BF16)
    wq_dT_d = mk.dram("wq_dT_d", (I, D), BF16)

    ident_f = mk.sb("ident_f", (128, 128), F32)
    ident_b = mk.sb("ident_b", (128, 128), BF16)
    win_ch = [mk.sb(f"win_ch{i}", (128, 8 * 128), BF16) for i in range(2)]
    wdt_ch = win_ch
    wout_ph = mk.sb("wout_ph", (128, 8 * D), BF16)
    wbcT = mk.sb("wbcT", (128, 8 * 2 * N), BF16)
    wg_ch = mk.sb("wg_chA", (128, 8 * ICW), BF16)
    wu_ch = mk.sb("wu_chA", (128, 8 * ICW), BF16)
    wd_ch = mk.sb("wd_chA", (128, (ICW // 128) * D), BF16)
    convw = mk.sb("convw", (128, DG * K4), F32)
    convb = mk.sb("convb", (128, DG), F32)
    bdt = mk.sb("bdt", (128, DG), F32)
    bin_sb = mk.sb("bin_sb", (128, 16), F32)
    bbc = mk.sb("bbc", (64, 1), F32)
    A_sb = mk.sb("A_sb", (128, DG * N), F32)
    g_rep = mk.sb("g_rep", (128, D), BF16)
    b_rep = mk.sb("b_rep", (128, D), BF16)
    ob_rep = mk.sb("ob_rep", (128, D), BF16)
    carry = mk.sb("carry", (128, DG * N * B), BF16)

    xn_tm = mk.sb("xn_tm", (128, B * TPB * D), F32)
    xn_fm = mk.sb("xn_fm", (128, DG * W3), BF16)
    xc = mk.sb("xc", (128, W3), F32)
    xs = mk.sb("xs", (128, DG * W), BF16)
    sz = mk.sb("sz", (128, DG * W), BF16)
    dt_sb = mk.sb("dt_sb", (128, DG * W), BF16)
    dtx = mk.sb("dtx", (128, DG * W), BF16)
    bc_fm = mk.sb("bc_fm", (64, W), BF16)
    a_bufs = [mk.sb(f"a_buf{i}", (128, WC4), F32) for i in range(2)]
    u_bufs = [mk.sb(f"u_buf{i}", (128, WC4), BF16) for i in range(2)]
    V_bufs = [mk.sb(f"V_buf{i}", (128, 4 * W), BF16) for i in range(2)]
    Breps = [mk.sb(f"Brep{i}", (128, W), BF16) for i in range(2)]
    Creps = [mk.sb(f"Crep{i}", (128, W), BF16) for i in range(2)]
    Yacc = mk.sb("Yacc", (128, 4 * W), BF16)
    y_fm = sz
    xq_tm = mk.sb("xq_tm", (128, D), BF16)
    SCR_A = mk.sb("SCR_A", (128, D), F32)
    q_bm = mk.sb("q_bm", (128, 64), F32)
    q_sc = mk.sb("q_sc", (128, 64), F32)
    q_rs = mk.sb("q_rs", (128, 64), F32)
    q_v = mk.sb("q_v", (128, D), F32)
    q_p1 = mk.sb("q_p1", (128, D), F32)
    q_p2 = mk.sb("q_p2", (128, D), F32)
    q_p3 = mk.sb("q_p3", (128, D), F32)
    qscr = dict(bm=q_bm, sc=q_sc, rs=q_rs, v=q_v[:], p1=q_p1[:], p2=q_p2[:], p3=q_p3[:])
    st_sum = mk.sb("st_sum", (128, 1), F32)
    st_sq = mk.sb("st_sq", (128, 1), F32)
    st_m = mk.sb("st_m", (128, 1), F32)
    st_r = mk.sb("st_r", (128, 1), F32)
    lnscr = dict(st_sum=st_sum, st_sq=st_sq, st_m=st_m, st_r=st_r, sq=q_v[:])
    h_sb = mk.sb("h_sb", (128, 512), F32)
    hq_tm = mk.sb("hq_tm", (128, 512), BF16)
    hq_fm = mk.sb("hq_fm", (128, 4 * 128), BF16)
    dacc = mk.sb("dacc", (128, (W // 128) * D), BF16)
    attn_sb = mk.sb("attn_sb", (128, D), F32)
    xt = attn_sb
    xn1 = SCR_A[0:1, :D]
    xres = q_p2[:, :D]
    o_sb = mk.sb("o_sb", (128, D), F32)
    cacc = q_v[:, :W] if W <= D else mk.sb("cacc", (128, W), F32)
    sp1 = q_p1[:, :512]
    sp2 = q_p2[:, :512]
    sp3 = q_p3[:, :512]
    hs_sb = xc[:, :512] if W3 >= 512 else mk.sb("hs_sb", (128, 512), F32)
    WS = SCR_A
    WPREP = sz[:, :2048]
    WQS = WPREP[:, :1024]
    WTS = WPREP[:, 1024:2048]

    ps1 = mk.ps("ps1", (128, 512))
    ps2 = mk.ps("ps2", (128, 512))
    ps_g = mk.ps("ps_g", (128, 512))
    ps_y = mk.ps("ps_y", (128, 512))
    ps_o = mk.ps("ps_o", (128, 512))
    ps_g2 = mk.ps("ps_g2", (128, 512))
    ps_y2 = mk.ps("ps_y2", (128, 512))
    ps_tb = mk.ps("ps_tb", (128, 128), BF16)

    V, S, G, TMM, TTR = mk.v, mk.s, mk.g, mk.t_mm, mk.t_tr
    dma = p.dma

    dma("sync", ident_f[:], ident_in[:], "prep")
    V(lambda e: nc.vector.tensor_copy(ident_b[:], ident_f[:]), (ident_f,), (ident_b,), "prep")

    def cast_to_bounce(src, dst, ncols, tag):
        for kk in range(8):
            c0 = 0
            while c0 < ncols:
                cw = min(D, ncols - c0)
                stv = WS[:, :cw]
                dma("sync", stv, src[kk * 128:(kk + 1) * 128, c0:c0 + cw], tag)
                qv = WQS[:, :cw]
                V(lambda e, qv=qv, stv=stv: nc.vector.tensor_copy(qv, stv), (SCR_A,), (sz,), tag)
                dma("scalar", dst[kk * 128:(kk + 1) * 128, c0:c0 + cw], qv, tag)
                c0 += cw

    cast_to_bounce(w_inT, winT_d, 2 * D, "prep:win")
    cast_to_bounce(w_dtT, wdtT_d, D, "prep:wdt")
    cast_to_bounce(w_outT, woutT_d, D, "prep:wout")

    for kk in range(8):
        stv = WS[:, :2 * N]
        dma("sync", stv, w_bcT[kk * 128:(kk + 1) * 128, :], "prep:wbc")
        dstv = wbcT[:, kk * 2 * N:(kk + 1) * 2 * N]
        V(lambda e, dstv=dstv, stv=stv: nc.vector.tensor_copy(dstv, stv), (SCR_A,), (wbcT,), "prep")

    for dg in range(DG):
        dma("sync", convw[:, dg * K4:(dg + 1) * K4], conv_w[dg * 128:(dg + 1) * 128, :], "prep")
        dma("sync", convb[:, dg:dg + 1], conv_b[dg * 128:(dg + 1) * 128].unsqueeze(1), "prep")
        dma("sync", bdt[:, dg:dg + 1], b_dt[dg * 128:(dg + 1) * 128].unsqueeze(1), "prep")
        stv = WS[:, :N]
        dma("sync", stv, A_log[dg * 128:(dg + 1) * 128, :], "prep")
        a_d = A_sb[:, dg * N:(dg + 1) * N]
        V(lambda e, a_d=a_d, stv=stv: nc.vector.tensor_scalar(a_d, stv, -5.0, 2.0, OP.max, OP.min),
          (SCR_A,), (A_sb,), "prep")
        S(lambda e, a_d=a_d: nc.scalar.activation(a_d, a_d, AF.Exp), (A_sb,), (A_sb,), "prep")
        V(lambda e, a_d=a_d: nc.vector.tensor_scalar(a_d, a_d, -1.0, None, OP.mult),
          (A_sb,), (A_sb,), "prep")
    for jg in range(16):
        dma("sync", bin_sb[:, jg:jg + 1], b_in[jg * 128:(jg + 1) * 128].unsqueeze(1), "prep")
    dma("sync", bbc[:], b_bc[:].unsqueeze(1), "prep")
    for src_d, dst_sb in ((ln_g, g_rep), (ln_b, b_rep), (b_out, ob_rep)):
        stv = WS[:, :D]
        dma("sync", stv, src_d[:].unsqueeze(0).broadcast_to([128, D]), "prep")
        V(lambda e, dst_sb=dst_sb, stv=stv: nc.vector.tensor_copy(dst_sb[:], stv),
          (SCR_A,), (dst_sb,), "prep")
    G(lambda e: nc.gpsimd.memset(carry[:], 0.0), (),
      tuple([(carry, "carry%d" % (i * 4 * B)) for i in range(2 * N)]), "prep")
    G(lambda e: nc.gpsimd.memset(a_bufs[0][:], 0.0), (), (a_bufs[0],), "prep")
    G(lambda e: nc.gpsimd.memset(a_bufs[1][:], 0.0), (), (a_bufs[1],), "prep")

    def prep_ffn_weight(w_src, wq_dst, nrows, ncols, tag):
        for rg in range(nrows // 128):
            c0 = 0
            while c0 < ncols:
                cw = min(D, ncols - c0)
                stv = WS[:, :cw]
                dma("sync", stv, w_src[rg * 128:(rg + 1) * 128, c0:c0 + cw], tag)
                qd = WQS[:, :cw]
                mk.emit_quant(stv, qd, qscr, cw // 16, tag)
                for it in range(cw // 128):
                    TTR(ps_tb[:], qd[:, it * 128:(it + 1) * 128], ident_b[:], tag)
                    dtile = WTS[:, it * 128:(it + 1) * 128]
                    V(lambda e, dtile=dtile: nc.vector.tensor_copy(dtile, ps_tb[:]),
                      (ps_tb,), (sz,), tag)
                o = wq_dst[c0:c0 + cw, rg * 128:(rg + 1) * 128].rearrange(
                    "(it p) c -> p it c", p=128)
                i_ = WTS[:, :cw].rearrange("p (it c) -> p it c", c=128)
                dma("scalar", o, i_, tag)
                c0 += cw

    prep_ffn_weight(w_g, wq_gT_d, I, D, "prep:wg")
    prep_ffn_weight(w_u, wq_uT_d, I, D, "prep:wu")
    prep_ffn_weight(w_d, wq_dT_d, D, I, "prep:wd")

    NTG = W // 128

    for ph in range(NPH):
        t0 = ph * T
        tagp = f"p{ph}"

        if ph > 0:
            for dg in range(DG):
                w3v = xn_fm[:, dg * W3:(dg + 1) * W3].rearrange("p (b c) -> p b c", c=T + 3)
                src = w3v[:, :, T:T + 2]
                dst = w3v[:, :, 0:2]
                V(lambda e, dst=dst, src=src: nc.vector.tensor_copy(dst, src),
                  (xn_fm,), (xn_fm,), tagp + ":tail")
        else:
            G(lambda e: nc.gpsimd.memset(xn_fm[:], 0.0), (), (xn_fm,), tagp)

        for b in range(B):
            for j in range(TPB):
                ta = t0 + j * 128
                dma("sync", xt[:], x_in[b, ta:ta + 128, :], tagp + ":x")
                xnt = xn_tm[:, (b * TPB + j) * D:(b * TPB + j + 1) * D]
                mk.emit_ln(xt[:], xnt, lnscr, tag=tagp + ":ln")
                for dg in range(DG):
                    TTR(ps1[:, 0:128], xnt[:, dg * 128:(dg + 1) * 128], ident_f[:], tagp + ":tr")
                    c0 = dg * W3 + b * (T + 3) + 2 + j * 128
                    dcol = xn_fm[:, c0:c0 + 128]
                    V(lambda e, dcol=dcol: nc.vector.tensor_copy(dcol, ps1[:, 0:128]),
                      (ps1,), (xn_fm,), tagp + ":tre")
            if ph < NPH - 1:
                dma("sync", xt[0:1, :], x_in[b, t0 + T:t0 + T + 1, :], tagp + ":x1")
                mk.emit_ln(xt[0:1, :], xn1[:], lnscr, tag=tagp + ":ln1")
                for dg in range(DG):
                    TTR(ps1[0:128, 0:1], xn1[:, dg * 128:(dg + 1) * 128], ident_f[0:1, 0:1], tagp + ":tr1")
                    c0 = dg * W3 + b * (T + 3) + 2 + T
                    dcol = xn_fm[:, c0:c0 + 1]
                    V(lambda e, dcol=dcol: nc.vector.tensor_copy(dcol, ps1[:, 0:1]),
                      (ps1,), (xn_fm,), tagp + ":tre1")

        for jg in range(16):
            is_xc = jg < 8
            dg = jg % 8
            wch = win_ch[jg % 2]
            wsrc = winT_d[:, jg * 128:(jg + 1) * 128].rearrange("(kk p) c -> p kk c", p=128)
            dma("sync", wch[:].rearrange("p (kk c) -> p kk c", c=128), wsrc, tagp + ":winl")
            bias = bin_sb[:, jg:jg + 1]
            if is_xc:
                for pr in range(2):
                    pp = (ps1, ps2)[pr]
                    ncol = 2 * (T + 3)
                    for kk in range(8):
                        lhs = wch[:, kk * 128:(kk + 1) * 128]
                        rhs = xn_fm[:, kk * W3 + pr * ncol:kk * W3 + (pr + 1) * ncol]
                        TMM(pp[:, 0:ncol], lhs, rhs, start=(kk == 0), stop=(kk == 7), tag=tagp + ":inp")
                    dst = xc[:, pr * ncol:(pr + 1) * ncol]
                    V(lambda e, dst=dst, pp=pp, ncol=ncol, bias=bias:
                      nc.vector.tensor_scalar(dst, pp[:, 0:ncol], bias, None, OP.add),
                      (pp, bin_sb, wch), (xc,), tagp + ":xce")
                if ph == 0:
                    for b in range(B):
                        pad = xc[:, b * (T + 3):b * (T + 3) + 2]
                        V(lambda e, pad=pad: nc.vector.memset(pad, 0.0), (), (xc,), tagp)
                if ph == NPH - 1:
                    for b in range(B):
                        pad = xc[:, b * (T + 3) + T + 2:b * (T + 3) + T + 3]
                        V(lambda e, pad=pad: nc.vector.memset(pad, 0.0), (), (xc,), tagp)
                cv = cacc[:].rearrange("p (b c) -> p b c", c=T)
                wv = convw[:, dg * K4:dg * K4 + 1]
                x_sh = xc[:].rearrange("p (b c) -> p b c", c=T + 3)[:, :, 0:T]
                V(lambda e, cv=cv, x_sh=x_sh, wv=wv:
                  nc.vector.tensor_scalar(cv, x_sh, wv, None, OP.mult),
                  (xc, convw), (cacc,), tagp + ":cv0")
                for k in range(1, K4):
                    wv = convw[:, dg * K4 + k:dg * K4 + k + 1]
                    x_sh = xc[:].rearrange("p (b c) -> p b c", c=T + 3)[:, :, k:k + T]
                    V(lambda e, cv=cv, x_sh=x_sh, wv=wv:
                      nc.vector.scalar_tensor_tensor(cv, x_sh, wv, cv, OP.mult, OP.add),
                      (xc, convw, cacc), (cacc,), tagp + f":cv{k}")
                xs_d = xs[:, dg * W:(dg + 1) * W]
                cb = convb[:, dg:dg + 1]
                S(lambda e, xs_d=xs_d, cb=cb:
                  nc.scalar.activation(xs_d, cacc[:], AF.Silu, bias=cb),
                  (cacc, convb), (xs,), tagp + ":xs")
            else:
                pp = (ps1, ps2)[jg % 2]
                ppv = pp[:, 0:W].rearrange("p (b t) -> p b t", t=T)
                for kk in range(8):
                    lhs = wch[:, kk * 128:(kk + 1) * 128]
                    rhs = xn_fm[:, kk * W3:(kk + 1) * W3].rearrange(
                        "p (b c) -> p b c", c=T + 3)[:, :, 2:2 + T]
                    TMM(ppv, lhs, rhs, start=(kk == 0), stop=(kk == 7), tag=tagp + ":zmm")
                dst = sz[:, dg * W:(dg + 1) * W]
                S(lambda e, dst=dst, pp=pp, bias=bias:
                  nc.scalar.activation(dst, pp[:, 0:W], AF.Silu, bias=bias),
                  (pp, bin_sb, wch), (sz,), tagp + ":z")

        for dgo in range(DG):
            wch = wdt_ch[dgo % 2]
            wsrc = wdtT_d[:, dgo * 128:(dgo + 1) * 128].rearrange("(kk p) c -> p kk c", p=128)
            dma("sync", wch[:].rearrange("p (kk c) -> p kk c", c=128), wsrc, tagp + ":wdtl")
            pp = (ps1, ps2)[dgo % 2]
            for kk in range(8):
                lhs = wch[:, kk * 128:(kk + 1) * 128]
                rhs = xs[:, kk * W:kk * W + W]
                TMM(pp[:, 0:W], lhs, rhs, start=(kk == 0), stop=(kk == 7), tag=tagp + ":dtm")
            bias = bdt[:, dgo:dgo + 1]
            V(lambda e, pp=pp, bias=bias:
              nc.vector.tensor_scalar(sp1[:, :W], pp[:, :W], bias, 0.0, OP.add, OP.max),
              (pp, bdt), (sp1,), tagp)
            V(lambda e, pp=pp, bias=bias:
              nc.vector.tensor_scalar(sp3[:, :W], pp[:, :W], bias, None, OP.add),
              (pp, bdt), (sp3,), tagp)
            V(lambda e: nc.vector.scalar_tensor_tensor(sp2[:, :W], sp1[:, :W], -2.0, sp3[:, :W], OP.mult, OP.add),
              (sp1, sp3), (sp2,), tagp)
            S(lambda e: nc.scalar.activation(sp2[:, :W], sp2[:, :W], AF.Exp), (sp2,), (sp2,), tagp)
            S(lambda e: nc.scalar.activation(sp2[:, :W], sp2[:, :W], AF.Ln, bias=1.0), (sp2,), (sp2,), tagp)
            V(lambda e: nc.vector.tensor_tensor(sp1[:, :W], sp1[:, :W], sp2[:, :W], OP.add),
              (sp1, sp2), (sp1,), tagp)
            dtd = dt_sb[:, dgo * W:(dgo + 1) * W]
            V(lambda e, dtd=dtd: nc.vector.tensor_scalar(dtd, sp1[:, :W], 0.2, 0.001, OP.min, OP.max),
              (sp1, wch), (dt_sb,), tagp + ":dt")
            xsd = xs[:, dgo * W:(dgo + 1) * W]
            dxd = dtx[:, dgo * W:(dgo + 1) * W]
            V(lambda e, dxd=dxd, dtd=dtd, xsd=xsd: nc.vector.tensor_tensor(dxd, dtd, xsd, OP.mult),
              (dt_sb, xs), (dtx,), tagp + ":dtx")
        pp = ps1
        for kk in range(8):
            lhs = wbcT[:, kk * 2 * N:(kk + 1) * 2 * N]
            rhs = xs[:, kk * W:kk * W + W]
            TMM(pp[0:64, 0:W], lhs, rhs, start=(kk == 0), stop=(kk == 7), tag=tagp + ":bcm")
        V(lambda e: nc.vector.tensor_scalar(bc_fm[:, :W], pp[0:64, :W], bbc[:], None, OP.add),
          (pp, bbc), (bc_fm,), tagp + ":bce")
        dma("scalar", bc_bounce[:], bc_fm[:], tagp + ":bcb")

        for tg in range(NTG):
            xnt = xn_tm[:, tg * D:(tg + 1) * D]
            mk.emit_quant(xnt, xq_tm[:], qscr, D // 16, tagp + ":xq")
            for dg in range(DG):
                TTR(ps_tb[:], xq_tm[:, dg * 128:(dg + 1) * 128], ident_b[:], tagp + ":qtr")
                dcol = xs[:, dg * W + tg * 128:dg * W + (tg + 1) * 128]
                V(lambda e, dcol=dcol: nc.vector.tensor_copy(dcol, ps_tb[:]),
                  (ps_tb,), (xs,), tagp + ":qtre")
        xq_fm = xs

        seg = lambda buf: buf[:].rearrange("p (g b c) -> p g b c", b=B, c=WCB)

        def emit_uc(half, n):
            pi = n % 2
            u_buf = u_bufs[pi]
            u_C = seg(u_buf)[:, :, :, 0:1]
            cidx = (n * 2 + half) * 4 * B
            cv4 = carry[:, cidx:cidx + 4 * B].rearrange("p (g b o) -> p g b o", b=B, o=1)
            G(lambda e, cv4=cv4, u_C=u_C: nc.gpsimd.tensor_copy(u_C, cv4),
              ((carry, "carry%d" % cidx),), ((u_bufs[pi], "uc%d" % pi),), tagp + ":uc")

        def d_pre(half):
            G(lambda e: nc.gpsimd.memset(Yacc[:], 0.0), (), (Yacc,), tagp + ":ya0")
            emit_uc(half, 0)
            emit_uc(half, 1)

        def d_n(half, n):
            pi = n % 2
            a_buf, V_buf, Brep, Crep = a_bufs[pi], V_bufs[pi], Breps[pi], Creps[pi]
            u_buf = u_bufs[pi]
            ukm = ("um%d" % pi)
            ukc = ("uc%d" % pi)
            a_T = seg(a_buf)[:, :, :, 1:WCB]
            u_T = seg(u_buf)[:, :, :, 1:WCB]
            s_T = u_T
            s_L = seg(u_buf)[:, :, :, T:T + 1]
            cidx = (n * 2 + half) * 4 * B
            cv4 = carry[:, cidx:cidx + 4 * B].rearrange("p (g b o) -> p g b o", b=B, o=1)
            dma("sync", Brep[:], bc_bounce[n:n + 1, :].broadcast_to([128, W]), tagp + ":Br")
            dma("sync", Crep[:], bc_bounce[N + n:N + n + 1, :].broadcast_to([128, W]), tagp + ":Cr")
            for dgi in range(4):
                dg = half * 4 + dgi
                dtd = dt_sb[:, dg * W:(dg + 1) * W].rearrange("p (b t) -> p b t", t=T)
                aT1 = a_T[:, dgi]
                S(lambda e, dtd=dtd, dg=dg, n=n, aT1=aT1:
                  nc.scalar.activation(aT1, dtd, AF.Exp, scale=A_sb[:, dg * N + n:dg * N + n + 1]),
                  (dt_sb, A_sb), (a_buf,), tagp + ":a")
            dxh = dtx[:, half * 4 * W:(half * 4 + 4) * W].rearrange("p (g b t) -> p g b t", b=B, t=T)
            br4 = Brep[:].rearrange("p (b t) -> p b t", t=T).unsqueeze(1).broadcast_to([128, 4, B, T])
            V(lambda e, dxh=dxh, br4=br4, u_T=u_T: nc.vector.tensor_tensor(u_T, dxh, br4, OP.mult),
              (dtx, Brep), ((u_buf, ukm),), tagp + ":u")
            V(lambda e, a_buf=a_buf, u_buf=u_buf: nc.vector.tensor_tensor_scan(u_buf[:], a_buf[:], u_buf[:], 0.0, OP.mult, OP.add),
              (a_buf, (u_buf, ukm), (u_buf, ukc)), ((u_buf, ukm), (u_buf, ukc)), tagp + ":tts")
            G(lambda e, cv4=cv4, s_L=s_L: nc.gpsimd.tensor_copy(cv4, s_L),
              (((u_buf, ukm)),), ((carry, "carry%d" % cidx),), tagp + ":cs")
            if n + 2 < N:
                emit_uc(half, n + 2)
            cr4 = Crep[:].rearrange("p (b t) -> p b t", t=T).unsqueeze(1).broadcast_to([128, 4, B, T])
            Vv = V_buf[:].rearrange("p (g b t) -> p g b t", b=B, t=T)
            V(lambda e, cr4=cr4, Vv=Vv, s_T=s_T: nc.vector.tensor_tensor(Vv, s_T, cr4, OP.mult),
              (((u_buf, ukm)), Crep), (V_buf,), tagp + ":V")
            G(lambda e, V_buf=V_buf: nc.gpsimd.tensor_tensor(Yacc[:], Yacc[:], V_buf[:], OP.add),
              (Yacc, V_buf), (Yacc,), tagp + ":ya")

        def d_post(half):
            for dgi in range(4):
                dg = half * 4 + dgi
                yv = Yacc[:, dgi * W:(dgi + 1) * W]
                szd = sz[:, dg * W:(dg + 1) * W]
                V(lambda e, yv=yv, szd=szd: nc.vector.tensor_tensor(szd, yv, szd, OP.mult),
                  (Yacc, sz), (sz,), tagp + ":y")

        d_units = []
        for half in range(2):
            d_units.append(lambda half=half: d_pre(half))
            for n in range(N):
                d_units.append(lambda half=half, n=n: d_n(half, n))
            d_units.append(lambda half=half: d_post(half))

        def f_load(ci):
            ic0, icw = CHUNKS[ci]
            gsrc = wq_gT_d[:, ic0:ic0 + icw].rearrange("(kk p) c -> p kk c", p=128)
            dma("scalar", wg_ch[:].rearrange("p (kk c) -> p kk c", c=ICW)[:, :, :icw], gsrc, tagp + ":wgl")
            usrc = wq_uT_d[:, ic0:ic0 + icw].rearrange("(kk p) c -> p kk c", p=128)
            dma("scalar", wu_ch[:].rearrange("p (kk c) -> p kk c", c=ICW)[:, :, :icw], usrc, tagp + ":wul")
            dsrc = wq_dT_d[ic0:ic0 + icw, :].rearrange("(it p) c -> p it c", p=128)
            dma("scalar", wd_ch[:, :(icw // 128) * D].rearrange("p (it c) -> p it c", c=D), dsrc, tagp + ":wdl")

        def f_dacc0():
            G(lambda e: nc.gpsimd.memset(dacc[:], 0.0), (), (dacc,), tagp + ":da0")

        def f_mm(ci, tg):
            u = (ci * NTG + tg) % 2
            pg, py = (ps_g, ps_y) if u == 0 else (ps_g2, ps_y2)
            for kk in range(8):
                lhs = xq_fm[:, kk * W + tg * 128:kk * W + (tg + 1) * 128]
                TMM(pg[:, :ICW], lhs, wg_ch[:, kk * ICW:(kk + 1) * ICW],
                    start=(kk == 0), stop=(kk == 7), tag=tagp + ":gm")
            for kk in range(8):
                lhs = xq_fm[:, kk * W + tg * 128:kk * W + (tg + 1) * 128]
                TMM(py[:, :ICW], lhs, wu_ch[:, kk * ICW:(kk + 1) * ICW],
                    start=(kk == 0), stop=(kk == 7), tag=tagp + ":um")

        def f_post(ci, tg):
            ic0, icw = CHUNKS[ci]
            nit = icw // 128
            u = (ci * NTG + tg) % 2
            pg, py = (ps_g, ps_y) if u == 0 else (ps_g2, ps_y2)
            S(lambda e, icw=icw, pg=pg: nc.scalar.activation(hs_sb[:, :icw], pg[:, :icw], AF.Silu),
              (pg,), (hs_sb,), tagp + ":hs")
            V(lambda e, icw=icw, py=py: nc.vector.tensor_tensor(h_sb[:, :icw], hs_sb[:, :icw], py[:, :icw], OP.mult),
              (hs_sb, py, wg_ch, wu_ch), (h_sb,), tagp + ":h")
            mk.emit_quant(h_sb[:, :icw], hq_tm[:, :icw], qscr, icw // 16, tagp + ":hq")
            for it in range(nit):
                TTR(ps_tb[:], hq_tm[:, it * 128:(it + 1) * 128], ident_b[:], tagp + ":htr")
                dcol = hq_fm[:, it * 128:(it + 1) * 128]
                V(lambda e, dcol=dcol: nc.vector.tensor_copy(dcol, ps_tb[:]),
                  (ps_tb,), (hq_fm,), tagp + ":htre")
            for jc in range(2):
                for it in range(nit):
                    TMM(ps_o[:, 0:512], hq_fm[:, it * 128:(it + 1) * 128],
                        wd_ch[:, it * D + jc * 512:it * D + (jc + 1) * 512],
                        start=(it == 0), stop=(it == nit - 1), tag=tagp + ":dm")
                dv = dacc[:, tg * D + jc * 512:tg * D + (jc + 1) * 512]
                V(lambda e, dv=dv: nc.vector.tensor_tensor(dv, dv, ps_o[:, 0:512], OP.add),
                  (dacc, ps_o, wd_ch), (dacc,), tagp + ":dacc")

        def f_woutl():
            wsrc = woutT_d[:, :].rearrange("(kk p) c -> p kk c", p=128)
            dma("scalar", wout_ph[:].rearrange("p (kk c) -> p kk c", c=D), wsrc, tagp + ":wol")

        f_units = [lambda: f_load(0), f_dacc0, f_woutl]
        for ci in range(NCH):
            pend = None
            for tg in range(NTG):
                f_units.append(lambda ci=ci, tg=tg: f_mm(ci, tg))
                if pend is not None:
                    f_units.append(lambda p=pend: f_post(*p))
                pend = (ci, tg)
            f_units.append(lambda p=pend: f_post(*p))
            if ci + 1 < NCH:
                f_units.append(lambda ci=ci: f_load(ci + 1))

        INTERLEAVE = True
        if INTERLEAVE:
            npre = 3
            for k in range(npre):
                f_units[k]()
            f_done = npre
            nf, nd = len(f_units), len(d_units)
            for di, du in enumerate(d_units):
                du()
                want = npre + (di + 1) * (nf - npre) // nd
                while f_done < want:
                    f_units[f_done]()
                    f_done += 1
            while f_done < nf:
                f_units[f_done]()
                f_done += 1
        else:
            for du in d_units:
                du()
            for fu in f_units:
                fu()

        for tg in range(NTG):
            b, j = tg // TPB, tg % TPB
            for jc in range(2):
                for kk in range(DG):
                    lhs = y_fm[:, kk * W + tg * 128:kk * W + (tg + 1) * 128]
                    TMM(ps_o[:, 0:512], lhs,
                        wout_ph[:, kk * D + jc * 512:kk * D + (jc + 1) * 512],
                        start=(kk == 0), stop=(kk == 7), tag=tagp + ":om")
                ob_h = ob_rep[:, jc * 512:(jc + 1) * 512]
                at_h = attn_sb[:, jc * 512:(jc + 1) * 512]
                V(lambda e, ob_h=ob_h, at_h=at_h: nc.vector.tensor_tensor(at_h, ps_o[:, 0:512], ob_h, OP.add),
                  (ps_o, ob_rep, wout_ph), (attn_sb,), tagp + ":ob")
            xnt = xn_tm[:, tg * D:(tg + 1) * D]
            V(lambda e, xnt=xnt: nc.vector.tensor_tensor(attn_sb[:], attn_sb[:], xnt, OP.add),
              (attn_sb, xn_tm), (attn_sb,), tagp + ":resm")
            mk.emit_ln(attn_sb[:], attn_sb[:], lnscr, gamma=g_rep[:], beta=b_rep[:], tag=tagp + ":mln")
            ta = t0 + j * 128
            dma("sync", xres[:], x_in[b, ta:ta + 128, :], tagp + ":xr")
            V(lambda e: nc.vector.tensor_tensor(o_sb[:], attn_sb[:], xres[:], OP.add),
              (attn_sb, q_p2), (o_sb,), tagp + ":fin1")
            dv = dacc[:, tg * D:(tg + 1) * D]
            V(lambda e, dv=dv: nc.vector.tensor_tensor(o_sb[:], o_sb[:], dv, OP.add),
              (o_sb, dacc), (o_sb,), tagp + ":fin2")
            dma("scalar", out_t[b, ta:ta + 128, :], o_sb[:], tagp + ":out")

    V(lambda e: nc.vector.engine_nop(), (out_t,), (), "final")

    from contextlib import ExitStack
    with ExitStack() as es:
        _sem_n = [0]

        def sem_factory(key):
            _sem_n[0] += 1
            return es.enter_context(nc.semaphore(f"sm{_sem_n[0]}"))

        block = es.enter_context(nc.Block())
        p.build(block, sem_factory)
    nc.finalize()
    return nc


_NC_CACHE = {}


def _get_nc(B, L, T):
    key = (B, L, T)
    if key not in _NC_CACHE:
        _NC_CACHE[key] = build_core_program(B, L, T)
    return _NC_CACHE[key]


def make_in_map(inputs, xshard):
    c = lambda a: np.ascontiguousarray(np.asarray(a), dtype=np.float32)
    return dict(
        x=c(xshard),
        w_inT=c(np.asarray(inputs["L0_attn_in_proj_w"]).T),
        b_in=c(inputs["L0_attn_in_proj_b"]),
        conv_w=c(np.asarray(inputs["L0_attn_conv_w"]).reshape(D, K4)),
        conv_b=c(inputs["L0_attn_conv_b"]),
        A_log=c(inputs["L0_attn_A_log"]),
        w_bcT=c(np.concatenate([np.asarray(inputs["L0_attn_B_w"]).T,
                                np.asarray(inputs["L0_attn_C_w"]).T], axis=1)),
        b_bc=c(np.concatenate([np.asarray(inputs["L0_attn_B_b"]),
                               np.asarray(inputs["L0_attn_C_b"])])),
        w_dtT=c(np.asarray(inputs["L0_attn_dt_w"]).T),
        b_dt=c(inputs["L0_attn_dt_b"]),
        w_outT=c(np.asarray(inputs["L0_attn_out_w"]).T),
        b_out=c(inputs["L0_attn_out_b"]),
        ln_g=c(inputs["L0_attn_ln_g"]),
        ln_b=c(inputs["L0_attn_ln_beta"]),
        w_g=c(inputs["L0_ffn_w_gate"]),
        w_u=c(inputs["L0_ffn_w_up"]),
        w_d=c(inputs["L0_ffn_w_down"]),
        ident=np.eye(128, dtype=np.float32),
    )


def kernel(**inputs):
    x = np.asarray(inputs["L0_attn_x"])
    NB = x.shape[0]
    ncores = 8
    bpc = NB // ncores
    nc = _get_nc(bpc, x.shape[1], 128)
    in_maps = [make_in_map(inputs, x[i * bpc:(i + 1) * bpc]) for i in range(ncores)]
    res = run_bass_kernel_spmd(nc, in_maps, core_ids=list(range(ncores)))
    out = np.concatenate([np.asarray(r["out"]) for r in res.results], axis=0)
    return out.astype(np.float32)


# version 17 (final)
# speedup vs baseline: 1.0424x; 1.0038x over previous
import numpy as np
import concourse.bass as bass
import concourse.mybir as mybir
from concourse.bass_utils import run_bass_kernel_spmd

F32 = mybir.dt.float32
BF16 = mybir.dt.bfloat16
AF = mybir.ActivationFunctionType
OP = mybir.AluOpType
AX = mybir.AxisListType

D = 1024
N = 32
DG = 8
I = 2816
K4 = 4
EPS = 1e-5


class Prog:

    def __init__(self, nc):
        self.nc = nc
        self.ops = []
        self.last_write = {}
        self.reads_since = {}

    @staticmethod
    def _key(buf):
        if isinstance(buf, tuple):
            return buf[1]
        t = getattr(buf, "tensor", buf)
        return t.name if hasattr(t, "name") else str(t)

    def op(self, engine, emit, reads=(), writes=(), dma=False, tag=""):
        idx = len(self.ops)
        deps = set()
        for b in reads:
            w = self.last_write.get(self._key(b))
            if w is not None:
                deps.add(w)
        for b in writes:
            k = self._key(b)
            w = self.last_write.get(k)
            if w is not None:
                if not (dma and self.ops[w]["dma"]):
                    deps.add(w)
            for r in self.reads_since.get(k, ()):
                deps.add(r)
        self.ops.append(dict(engine=engine, emit=emit, deps=sorted(deps), dma=dma,
                             tag=tag, milestone=False, mcount=None, sem=None))
        for b in reads:
            self.reads_since.setdefault(self._key(b), []).append(idx)
        for b in writes:
            k = self._key(b)
            self.last_write[k] = idx
            self.reads_since[k] = []
        return idx

    def dma(self, queue, out, in_, tag=""):
        def emit(eng):
            return eng.dma_start(out, in_)
        idx = self.op(queue, emit, reads=(in_,), writes=(out,), dma=True, tag=tag)
        self.ops[idx]["dmakey"] = self._key(out)
        return idx

    def build(self, block, sem_factory):
        ops = self.ops
        for o in ops:
            for d in o["deps"]:
                ops[d]["milestone"] = True
        counts = {}
        for o in ops:
            if o["dma"]:
                key = ("dmab", o["dmakey"])
                counts[key] = counts.get(key, 0) + 1
                o["mcount"] = counts[key] * 16
                o["sem"] = key
                o["milestone"] = True
            elif o["milestone"]:
                key = o["engine"]
                counts[key] = counts.get(key, 0) + 1
                o["mcount"] = counts[key]
                o["sem"] = key
        sems = {}
        for o in ops:
            if o.get("sem") is not None and o["sem"] not in sems:
                sems[o["sem"]] = sem_factory(o["sem"])
        per_engine = {}
        for i, o in enumerate(ops):
            per_engine.setdefault(o["engine"], []).append(i)

        def emit_stream(eng_name, eng):
            waited = {}
            for i in per_engine.get(eng_name, []):
                o = ops[i]
                need = {}
                for d in o["deps"]:
                    dep = ops[d]
                    sk = dep["sem"]
                    need[sk] = max(need.get(sk, 0), dep["mcount"])
                for sk, val in sorted(need.items(), key=lambda kv: str(kv[0])):
                    if waited.get(sk, 0) < val:
                        eng.wait_ge(sems[sk], val)
                        waited[sk] = val
                inst = o["emit"](eng)
                if o["dma"]:
                    inst.then_inc(sems[o["sem"]], 16)
                elif o["milestone"]:
                    inst.then_inc(sems[o["sem"]], 1)

        @block.sync
        def _(eng):
            emit_stream("sync", eng)

        @block.tensor
        def _(eng):
            emit_stream("tensor", eng)

        @block.vector
        def _(eng):
            emit_stream("vector", eng)

        @block.scalar
        def _(eng):
            emit_stream("scalar", eng)

        @block.gpsimd
        def _(eng):
            emit_stream("gpsimd", eng)


class MK:
    def __init__(self, B=4, L=2048, T=128):
        assert L % T == 0 and T % 128 == 0
        self.B, self.L, self.T = B, L, T
        self.NPH = L // T
        self.TPB = T // 128
        self.W = B * T
        self.W3 = B * (T + 3)
        self.nc = bass.Bass("TRN2", target_bir_lowering=False, debug=False)
        self.p = Prog(self.nc)

    def sb(self, name, shape, dtype):
        return self.nc.alloc_sbuf_tensor(name, list(shape), dtype).ap()

    def ps(self, name, shape, dtype=F32):
        return self.nc.alloc_psum_tensor(name, list(shape), dtype).ap()

    def dram(self, name, shape, dtype, kind="Internal"):
        return self.nc.dram_tensor(name, list(shape), dtype, kind=kind).ap()

    def t_mm(self, out, lhsT, rhs, start, stop, tag="mm"):
        self.p.op("tensor",
                  lambda e: self.nc.tensor.matmul(out, lhsT, rhs, start=start, stop=stop),
                  reads=(lhsT, rhs), writes=(out,), tag=tag)

    def t_tr(self, out, in_, ident, tag="tr"):
        self.p.op("tensor", lambda e: self.nc.tensor.transpose(out, in_, ident),
                  reads=(in_, ident), writes=(out,), tag=tag)

    def v(self, fn, reads, writes, tag="v"):
        self.p.op("vector", fn, reads=reads, writes=writes, tag=tag)

    def s(self, fn, reads, writes, tag="s"):
        self.p.op("scalar", fn, reads=reads, writes=writes, tag=tag)

    def g(self, fn, reads, writes, tag="g"):
        self.p.op("gpsimd", fn, reads=reads, writes=writes, tag=tag)

    def emit_quant(self, x_ap, out_ap, scr, nblk, tag="q"):
        nc = self.nc
        P = x_ap.shape[0]
        F = nblk * 16
        bm, sc, rs = scr["bm"][:, :nblk], scr["sc"][:, :nblk], scr["rs"][:, :nblk]
        ax = scr["p1"][:, :F]
        av = scr["p2"][:, :F]
        lA = scr["p3"][:, :F]
        lB = scr["v"][:, :F]
        kb, kv, k1, k2, k3 = scr["bm"], scr["v"], scr["p1"], scr["p2"], scr["p3"]
        ksc, krs = scr["sc"], scr["rs"]
        V, S, G = self.v, self.s, self.g
        S(lambda e: nc.scalar.activation(ax, x_ap, AF.Abs), (x_ap,), (k1,), tag + ":ax")
        bm3 = ax.rearrange("p (b k) -> p b k", k=16)
        V(lambda e: nc.vector.reduce_max(bm, bm3, axis=AX.X), (k1,), (kb,), tag + ":bm")
        V(lambda e: nc.vector.tensor_scalar(sc, bm, float(1.0 / 6.0), 1e-12, OP.mult, OP.max),
          (kb,), (ksc,), tag)
        V(lambda e: nc.vector.reciprocal(rs, sc), (ksc,), (krs,), tag)
        rs_b = rs.unsqueeze(2).broadcast_to([P, nblk, 16])
        ax3 = ax.rearrange("p (b k) -> p b k", k=16)
        av3 = av.rearrange("p (b k) -> p b k", k=16)
        V(lambda e: nc.vector.tensor_tensor(av3, ax3, rs_b, OP.mult), (k1, krs), (k2,), tag)
        V(lambda e: nc.vector.tensor_scalar(lA, av, 0.25, None, OP.is_gt), (k2,), (k3,), tag)
        for th in (0.75, 1.25, 1.75):
            V(lambda e, th=th: nc.vector.scalar_tensor_tensor(lA, av, th, lA, OP.is_gt, OP.add),
              (k2, k3), (k3,), tag)
        V(lambda e: nc.vector.tensor_scalar(lB, av, 2.5, None, OP.is_gt), (k2,), (kv,), tag)
        V(lambda e: nc.vector.scalar_tensor_tensor(lB, av, 3.5, lB, OP.is_gt, OP.add),
          (k2, kv), (kv,), tag)
        V(lambda e: nc.vector.tensor_scalar(av, av, 5.0, 2.0, OP.is_gt, OP.mult), (k2,), (k2,), tag)
        S(lambda e: nc.scalar.activation(ax, x_ap, AF.Sign), (x_ap,), (k1,), tag + ":sg")
        V(lambda e: nc.vector.scalar_tensor_tensor(lA, lA, 0.5, lB, OP.mult, OP.add),
          (k3, kv), (k3,), tag)
        V(lambda e: nc.vector.tensor_tensor(lA, lA, av, OP.add), (k3, k2), (k3,), tag)
        G(lambda e: nc.gpsimd.tensor_tensor(lB, lA, ax, OP.mult), (k3, k1), (kv,), tag)
        sc_b = sc.unsqueeze(2).broadcast_to([P, nblk, 16])
        q3 = lB.rearrange("p (b k) -> p b k", k=16)
        o3 = out_ap.rearrange("p (b k) -> p b k", k=16)
        V(lambda e: nc.vector.tensor_tensor(o3, q3, sc_b, OP.mult),
          (kv, ksc), (out_ap,), tag + ":dq")

    def emit_ln(self, x_ap, out_ap, scr, gamma=None, beta=None, tag="ln"):
        nc = self.nc
        P = x_ap.shape[0]
        F = x_ap.shape[-1]
        ssum, ssq = scr["st_sum"][:P], scr["st_sq"][:P]
        sm, sr = scr["st_m"][:P], scr["st_r"][:P]
        sq = scr["sq"][:P, :F]
        ks = scr
        self.v(lambda e: nc.vector.reduce_sum(ssum, x_ap, axis=AX.X), (x_ap,), (ks["st_sum"],), tag)
        self.s(lambda e: nc.scalar.activation(sq, x_ap, AF.Square, accum_out=ssq),
               (x_ap,), (ks["sq"], ks["st_sq"]), tag)
        self.v(lambda e: nc.vector.tensor_scalar(sm, ssum, 1.0 / F, None, OP.mult),
               (ks["st_sum"],), (ks["st_m"],), tag)
        self.v(lambda e: nc.vector.tensor_scalar(sr, ssq, 1.0 / F, EPS, OP.mult, OP.add),
               (ks["st_sq"],), (ks["st_r"],), tag)
        self.v(lambda e: nc.vector.scalar_tensor_tensor(ssq, sm, -1.0, sm, OP.mult, OP.mult),
               (ks["st_m"],), (ks["st_sq"],), tag)
        self.v(lambda e: nc.vector.tensor_tensor(sr, sr, ssq, OP.add),
               (ks["st_r"], ks["st_sq"]), (ks["st_r"],), tag)
        self.s(lambda e: nc.scalar.activation(sr, sr, AF.Sqrt), (ks["st_r"],), (ks["st_r"],), tag)
        self.v(lambda e: nc.vector.reciprocal(sr, sr), (ks["st_r"],), (ks["st_r"],), tag)
        self.v(lambda e: nc.vector.scalar_tensor_tensor(sm, sm, -1.0, sr, OP.mult, OP.mult),
               (ks["st_m"], ks["st_r"]), (ks["st_m"],), tag)
        self.v(lambda e: nc.vector.tensor_scalar(out_ap, x_ap, sr, sm, OP.mult, OP.add),
               (x_ap, ks["st_r"], ks["st_m"]), (out_ap,), tag + ":nrm")
        if gamma is not None:
            self.v(lambda e: nc.vector.tensor_tensor(out_ap, out_ap, gamma, OP.mult),
                   (out_ap, gamma), (out_ap,), tag)
            self.v(lambda e: nc.vector.tensor_tensor(out_ap, out_ap, beta, OP.add),
                   (out_ap, beta), (out_ap,), tag)


def build_core_program(B=4, L=2048, T=128):
    mk = MK(B, L, T)
    nc, p = mk.nc, mk.p
    W, W3, TPB, NPH = mk.W, mk.W3, mk.TPB, mk.NPH
    WCB = 1 + T
    WC4 = 4 * B * WCB
    ICW = 512
    CHUNKS = []
    _c0 = 0
    while _c0 < I:
        CHUNKS.append((_c0, min(ICW, I - _c0)))
        _c0 += ICW
    NCH = len(CHUNKS)

    x_in = mk.dram("x", (B, L, D), F32, kind="ExternalInput")
    out_t = mk.dram("out", (B, L, D), F32, kind="ExternalOutput")
    w_inT = mk.dram("w_inT", (D, 2 * D), F32, kind="ExternalInput")
    b_in = mk.dram("b_in", (2 * D,), F32, kind="ExternalInput")
    conv_w = mk.dram("conv_w", (D, K4), F32, kind="ExternalInput")
    conv_b = mk.dram("conv_b", (D,), F32, kind="ExternalInput")
    A_log = mk.dram("A_log", (D, N), F32, kind="ExternalInput")
    w_bcT = mk.dram("w_bcT", (D, 2 * N), F32, kind="ExternalInput")
    b_bc = mk.dram("b_bc", (2 * N,), F32, kind="ExternalInput")
    w_dtT = mk.dram("w_dtT", (D, D), F32, kind="ExternalInput")
    b_dt = mk.dram("b_dt", (D,), F32, kind="ExternalInput")
    w_outT = mk.dram("w_outT", (D, D), F32, kind="ExternalInput")
    b_out = mk.dram("b_out", (D,), F32, kind="ExternalInput")
    ln_g = mk.dram("ln_g", (D,), F32, kind="ExternalInput")
    ln_b = mk.dram("ln_b", (D,), F32, kind="ExternalInput")
    w_g = mk.dram("w_g", (I, D), F32, kind="ExternalInput")
    w_u = mk.dram("w_u", (I, D), F32, kind="ExternalInput")
    w_d = mk.dram("w_d", (D, I), F32, kind="ExternalInput")
    ident_in = mk.dram("ident", (128, 128), F32, kind="ExternalInput")

    bc_bounce = mk.dram("bc_bounce", (2 * N, W), BF16)
    winT_d = mk.dram("winT_d", (D, 2 * D), BF16)
    wdtT_d = mk.dram("wdtT_d", (D, D), BF16)
    woutT_d = mk.dram("woutT_d", (D, D), BF16)
    wq_gT_d = mk.dram("wq_gT_d", (D, I), BF16)
    wq_uT_d = mk.dram("wq_uT_d", (D, I), BF16)
    wq_dT_d = mk.dram("wq_dT_d", (I, D), BF16)

    ident_f = mk.sb("ident_f", (128, 128), F32)
    ident_b = mk.sb("ident_b", (128, 128), BF16)
    win_ch = [mk.sb(f"win_ch{i}", (128, 8 * 128), BF16) for i in range(2)]
    wdt_ch = win_ch
    wout_ph = mk.sb("wout_ph", (128, 8 * D), BF16)
    wbcT = mk.sb("wbcT", (128, 8 * 2 * N), BF16)
    wg_ch = mk.sb("wg_chA", (128, 8 * ICW), BF16)
    wu_ch = mk.sb("wu_chA", (128, 8 * ICW), BF16)
    wd_ch = mk.sb("wd_chA", (128, (ICW // 128) * D), BF16)
    convw = mk.sb("convw", (128, DG * K4), F32)
    convb = mk.sb("convb", (128, DG), F32)
    bdt = mk.sb("bdt", (128, DG), F32)
    bin_sb = mk.sb("bin_sb", (128, 16), F32)
    bbc = mk.sb("bbc", (64, 1), F32)
    A_sb = mk.sb("A_sb", (128, DG * N), F32)
    g_rep = mk.sb("g_rep", (128, D), BF16)
    b_rep = mk.sb("b_rep", (128, D), BF16)
    ob_rep = mk.sb("ob_rep", (128, D), BF16)
    carry = mk.sb("carry", (128, DG * N * B), BF16)

    xn_tm = mk.sb("xn_tm", (128, B * TPB * D), F32)
    xn_fm = mk.sb("xn_fm", (128, DG * W3), BF16)
    xc = mk.sb("xc", (128, W3), F32)
    xs = mk.sb("xs", (128, DG * W), BF16)
    sz = mk.sb("sz", (128, DG * W), BF16)
    dt_sb = mk.sb("dt_sb", (128, DG * W), BF16)
    dtx = mk.sb("dtx", (128, DG * W), BF16)
    bc_fm = mk.sb("bc_fm", (64, W), BF16)
    a_bufs = [mk.sb(f"a_buf{i}", (128, WC4), F32) for i in range(2)]
    u_bufs = [mk.sb(f"u_buf{i}", (128, WC4), BF16) for i in range(2)]
    V_bufs = [mk.sb(f"V_buf{i}", (128, 4 * W), BF16) for i in range(2)]
    Breps = [mk.sb(f"Brep{i}", (128, W), BF16) for i in range(2)]
    Creps = [mk.sb(f"Crep{i}", (128, W), BF16) for i in range(2)]
    Yacc = mk.sb("Yacc", (128, 4 * W), BF16)
    y_fm = sz
    xq_tm = mk.sb("xq_tm", (128, D), BF16)
    SCR_A = mk.sb("SCR_A", (128, D), F32)
    q_bm = mk.sb("q_bm", (128, 64), F32)
    q_sc = mk.sb("q_sc", (128, 64), F32)
    q_rs = mk.sb("q_rs", (128, 64), F32)
    q_v = mk.sb("q_v", (128, D), F32)
    q_p1 = mk.sb("q_p1", (128, D), F32)
    q_p2 = mk.sb("q_p2", (128, D), F32)
    q_p3 = mk.sb("q_p3", (128, D), F32)
    qscr = dict(bm=q_bm, sc=q_sc, rs=q_rs, v=q_v[:], p1=q_p1[:], p2=q_p2[:], p3=q_p3[:])
    st_sum = mk.sb("st_sum", (128, 1), F32)
    st_sq = mk.sb("st_sq", (128, 1), F32)
    st_m = mk.sb("st_m", (128, 1), F32)
    st_r = mk.sb("st_r", (128, 1), F32)
    lnscr = dict(st_sum=st_sum, st_sq=st_sq, st_m=st_m, st_r=st_r, sq=q_v[:])
    h_sb = mk.sb("h_sb", (128, 512), F32)
    hq_tm = mk.sb("hq_tm", (128, 512), BF16)
    hq_fm = mk.sb("hq_fm", (128, 4 * 128), BF16)
    dacc = mk.sb("dacc", (128, (W // 128) * D), BF16)
    attn_sb = mk.sb("attn_sb", (128, D), F32)
    xt = attn_sb
    xn1 = SCR_A[0:1, :D]
    xres = q_p2[:, :D]
    o_sb = mk.sb("o_sb", (128, D), F32)
    cacc = q_v[:, :W] if W <= D else mk.sb("cacc", (128, W), F32)
    sp1 = q_p1[:, :512]
    sp2 = q_p2[:, :512]
    sp3 = q_p3[:, :512]
    hs_sb = xc[:, :512] if W3 >= 512 else mk.sb("hs_sb", (128, 512), F32)
    WS = SCR_A
    WPREP = sz[:, :2048]
    WQS = WPREP[:, :1024]
    WTS = WPREP[:, 1024:2048]

    ps1 = mk.ps("ps1", (128, 512))
    ps2 = mk.ps("ps2", (128, 512))
    ps_g = mk.ps("ps_g", (128, 512))
    ps_y = mk.ps("ps_y", (128, 512))
    ps_o = mk.ps("ps_o", (128, 512))
    ps_g2 = mk.ps("ps_g2", (128, 512))
    ps_y2 = mk.ps("ps_y2", (128, 512))
    ps_tb = mk.ps("ps_tb", (128, 128), BF16)

    V, S, G, TMM, TTR = mk.v, mk.s, mk.g, mk.t_mm, mk.t_tr
    dma = p.dma

    dma("sync", ident_f[:], ident_in[:], "prep")
    V(lambda e: nc.vector.tensor_copy(ident_b[:], ident_f[:]), (ident_f,), (ident_b,), "prep")

    def cast_to_bounce(src, dst, ncols, tag):
        for kk in range(8):
            c0 = 0
            while c0 < ncols:
                cw = min(D, ncols - c0)
                stv = WS[:, :cw]
                dma("sync", stv, src[kk * 128:(kk + 1) * 128, c0:c0 + cw], tag)
                qv = WQS[:, :cw]
                V(lambda e, qv=qv, stv=stv: nc.vector.tensor_copy(qv, stv), (SCR_A,), (sz,), tag)
                dma("scalar", dst[kk * 128:(kk + 1) * 128, c0:c0 + cw], qv, tag)
                c0 += cw

    cast_to_bounce(w_inT, winT_d, 2 * D, "prep:win")
    cast_to_bounce(w_dtT, wdtT_d, D, "prep:wdt")
    cast_to_bounce(w_outT, woutT_d, D, "prep:wout")

    for kk in range(8):
        stv = WS[:, :2 * N]
        dma("sync", stv, w_bcT[kk * 128:(kk + 1) * 128, :], "prep:wbc")
        dstv = wbcT[:, kk * 2 * N:(kk + 1) * 2 * N]
        V(lambda e, dstv=dstv, stv=stv: nc.vector.tensor_copy(dstv, stv), (SCR_A,), (wbcT,), "prep")

    for dg in range(DG):
        dma("sync", convw[:, dg * K4:(dg + 1) * K4], conv_w[dg * 128:(dg + 1) * 128, :], "prep")
        dma("sync", convb[:, dg:dg + 1], conv_b[dg * 128:(dg + 1) * 128].unsqueeze(1), "prep")
        dma("sync", bdt[:, dg:dg + 1], b_dt[dg * 128:(dg + 1) * 128].unsqueeze(1), "prep")
        stv = WS[:, :N]
        dma("sync", stv, A_log[dg * 128:(dg + 1) * 128, :], "prep")
        a_d = A_sb[:, dg * N:(dg + 1) * N]
        V(lambda e, a_d=a_d, stv=stv: nc.vector.tensor_scalar(a_d, stv, -5.0, 2.0, OP.max, OP.min),
          (SCR_A,), (A_sb,), "prep")
        S(lambda e, a_d=a_d: nc.scalar.activation(a_d, a_d, AF.Exp), (A_sb,), (A_sb,), "prep")
        V(lambda e, a_d=a_d: nc.vector.tensor_scalar(a_d, a_d, -1.0, None, OP.mult),
          (A_sb,), (A_sb,), "prep")
    for jg in range(16):
        dma("sync", bin_sb[:, jg:jg + 1], b_in[jg * 128:(jg + 1) * 128].unsqueeze(1), "prep")
    dma("sync", bbc[:], b_bc[:].unsqueeze(1), "prep")
    for src_d, dst_sb in ((ln_g, g_rep), (ln_b, b_rep), (b_out, ob_rep)):
        stv = WS[:, :D]
        dma("sync", stv, src_d[:].unsqueeze(0).broadcast_to([128, D]), "prep")
        V(lambda e, dst_sb=dst_sb, stv=stv: nc.vector.tensor_copy(dst_sb[:], stv),
          (SCR_A,), (dst_sb,), "prep")
    G(lambda e: nc.gpsimd.memset(carry[:], 0.0), (),
      tuple([(carry, "carry%d" % (i * 4 * B)) for i in range(2 * N)]), "prep")
    G(lambda e: nc.gpsimd.memset(a_bufs[0][:], 0.0), (), (a_bufs[0],), "prep")
    G(lambda e: nc.gpsimd.memset(a_bufs[1][:], 0.0), (), (a_bufs[1],), "prep")

    def prep_ffn_weight(w_src, wq_dst, nrows, ncols, tag):
        for rg in range(nrows // 128):
            c0 = 0
            while c0 < ncols:
                cw = min(D, ncols - c0)
                stv = WS[:, :cw]
                dma("sync", stv, w_src[rg * 128:(rg + 1) * 128, c0:c0 + cw], tag)
                qd = WQS[:, :cw]
                mk.emit_quant(stv, qd, qscr, cw // 16, tag)
                for it in range(cw // 128):
                    TTR(ps_tb[:], qd[:, it * 128:(it + 1) * 128], ident_b[:], tag)
                    dtile = WTS[:, it * 128:(it + 1) * 128]
                    V(lambda e, dtile=dtile: nc.vector.tensor_copy(dtile, ps_tb[:]),
                      (ps_tb,), (sz,), tag)
                o = wq_dst[c0:c0 + cw, rg * 128:(rg + 1) * 128].rearrange(
                    "(it p) c -> p it c", p=128)
                i_ = WTS[:, :cw].rearrange("p (it c) -> p it c", c=128)
                dma("scalar", o, i_, tag)
                c0 += cw

    prep_ffn_weight(w_g, wq_gT_d, I, D, "prep:wg")
    prep_ffn_weight(w_u, wq_uT_d, I, D, "prep:wu")
    prep_ffn_weight(w_d, wq_dT_d, D, I, "prep:wd")

    NTG = W // 128

    for ph in range(NPH):
        t0 = ph * T
        tagp = f"p{ph}"

        if ph > 0:
            for dg in range(DG):
                w3v = xn_fm[:, dg * W3:(dg + 1) * W3].rearrange("p (b c) -> p b c", c=T + 3)
                src = w3v[:, :, T:T + 2]
                dst = w3v[:, :, 0:2]
                V(lambda e, dst=dst, src=src: nc.vector.tensor_copy(dst, src),
                  (xn_fm,), (xn_fm,), tagp + ":tail")
        else:
            G(lambda e: nc.gpsimd.memset(xn_fm[:], 0.0), (), (xn_fm,), tagp)

        for b in range(B):
            for j in range(TPB):
                ta = t0 + j * 128
                dma("sync", xt[:], x_in[b, ta:ta + 128, :], tagp + ":x")
                xnt = xn_tm[:, (b * TPB + j) * D:(b * TPB + j + 1) * D]
                mk.emit_ln(xt[:], xnt, lnscr, tag=tagp + ":ln")
                for dg in range(DG):
                    TTR(ps1[:, 0:128], xnt[:, dg * 128:(dg + 1) * 128], ident_f[:], tagp + ":tr")
                    c0 = dg * W3 + b * (T + 3) + 2 + j * 128
                    dcol = xn_fm[:, c0:c0 + 128]
                    V(lambda e, dcol=dcol: nc.vector.tensor_copy(dcol, ps1[:, 0:128]),
                      (ps1,), (xn_fm,), tagp + ":tre")
            if ph < NPH - 1:
                dma("sync", xt[0:1, :], x_in[b, t0 + T:t0 + T + 1, :], tagp + ":x1")
                mk.emit_ln(xt[0:1, :], xn1[:], lnscr, tag=tagp + ":ln1")
                for dg in range(DG):
                    TTR(ps1[0:128, 0:1], xn1[:, dg * 128:(dg + 1) * 128], ident_f[0:1, 0:1], tagp + ":tr1")
                    c0 = dg * W3 + b * (T + 3) + 2 + T
                    dcol = xn_fm[:, c0:c0 + 1]
                    V(lambda e, dcol=dcol: nc.vector.tensor_copy(dcol, ps1[:, 0:1]),
                      (ps1,), (xn_fm,), tagp + ":tre1")

        for jg in range(16):
            is_xc = jg < 8
            dg = jg % 8
            wch = win_ch[jg % 2]
            wsrc = winT_d[:, jg * 128:(jg + 1) * 128].rearrange("(kk p) c -> p kk c", p=128)
            dma("sync", wch[:].rearrange("p (kk c) -> p kk c", c=128), wsrc, tagp + ":winl")
            bias = bin_sb[:, jg:jg + 1]
            if is_xc:
                for pr in range(2):
                    pp = (ps1, ps2)[pr]
                    ncol = 2 * (T + 3)
                    for kk in range(8):
                        lhs = wch[:, kk * 128:(kk + 1) * 128]
                        rhs = xn_fm[:, kk * W3 + pr * ncol:kk * W3 + (pr + 1) * ncol]
                        TMM(pp[:, 0:ncol], lhs, rhs, start=(kk == 0), stop=(kk == 7), tag=tagp + ":inp")
                    dst = xc[:, pr * ncol:(pr + 1) * ncol]
                    V(lambda e, dst=dst, pp=pp, ncol=ncol, bias=bias:
                      nc.vector.tensor_scalar(dst, pp[:, 0:ncol], bias, None, OP.add),
                      (pp, bin_sb, wch), (xc,), tagp + ":xce")
                if ph == 0:
                    for b in range(B):
                        pad = xc[:, b * (T + 3):b * (T + 3) + 2]
                        V(lambda e, pad=pad: nc.vector.memset(pad, 0.0), (), (xc,), tagp)
                if ph == NPH - 1:
                    for b in range(B):
                        pad = xc[:, b * (T + 3) + T + 2:b * (T + 3) + T + 3]
                        V(lambda e, pad=pad: nc.vector.memset(pad, 0.0), (), (xc,), tagp)
                cv = cacc[:].rearrange("p (b c) -> p b c", c=T)
                wv = convw[:, dg * K4:dg * K4 + 1]
                x_sh = xc[:].rearrange("p (b c) -> p b c", c=T + 3)[:, :, 0:T]
                V(lambda e, cv=cv, x_sh=x_sh, wv=wv:
                  nc.vector.tensor_scalar(cv, x_sh, wv, None, OP.mult),
                  (xc, convw), (cacc,), tagp + ":cv0")
                for k in range(1, K4):
                    wv = convw[:, dg * K4 + k:dg * K4 + k + 1]
                    x_sh = xc[:].rearrange("p (b c) -> p b c", c=T + 3)[:, :, k:k + T]
                    V(lambda e, cv=cv, x_sh=x_sh, wv=wv:
                      nc.vector.scalar_tensor_tensor(cv, x_sh, wv, cv, OP.mult, OP.add),
                      (xc, convw, cacc), (cacc,), tagp + f":cv{k}")
                xs_d = xs[:, dg * W:(dg + 1) * W]
                cb = convb[:, dg:dg + 1]
                S(lambda e, xs_d=xs_d, cb=cb:
                  nc.scalar.activation(xs_d, cacc[:], AF.Silu, bias=cb),
                  (cacc, convb), (xs,), tagp + ":xs")
            else:
                pp = (ps1, ps2)[jg % 2]
                ppv = pp[:, 0:W].rearrange("p (b t) -> p b t", t=T)
                for kk in range(8):
                    lhs = wch[:, kk * 128:(kk + 1) * 128]
                    rhs = xn_fm[:, kk * W3:(kk + 1) * W3].rearrange(
                        "p (b c) -> p b c", c=T + 3)[:, :, 2:2 + T]
                    TMM(ppv, lhs, rhs, start=(kk == 0), stop=(kk == 7), tag=tagp + ":zmm")
                dst = sz[:, dg * W:(dg + 1) * W]
                S(lambda e, dst=dst, pp=pp, bias=bias:
                  nc.scalar.activation(dst, pp[:, 0:W], AF.Silu, bias=bias),
                  (pp, bin_sb, wch), (sz,), tagp + ":z")

        for dgo in range(DG):
            wch = wdt_ch[dgo % 2]
            wsrc = wdtT_d[:, dgo * 128:(dgo + 1) * 128].rearrange("(kk p) c -> p kk c", p=128)
            dma("sync", wch[:].rearrange("p (kk c) -> p kk c", c=128), wsrc, tagp + ":wdtl")
            pp = (ps1, ps2)[dgo % 2]
            for kk in range(8):
                lhs = wch[:, kk * 128:(kk + 1) * 128]
                rhs = xs[:, kk * W:kk * W + W]
                TMM(pp[:, 0:W], lhs, rhs, start=(kk == 0), stop=(kk == 7), tag=tagp + ":dtm")
            bias = bdt[:, dgo:dgo + 1]
            V(lambda e, pp=pp, bias=bias:
              nc.vector.tensor_scalar(sp1[:, :W], pp[:, :W], bias, 0.0, OP.add, OP.max),
              (pp, bdt), (sp1,), tagp)
            V(lambda e, pp=pp, bias=bias:
              nc.vector.tensor_scalar(sp3[:, :W], pp[:, :W], bias, None, OP.add),
              (pp, bdt), (sp3,), tagp)
            V(lambda e: nc.vector.scalar_tensor_tensor(sp2[:, :W], sp1[:, :W], -2.0, sp3[:, :W], OP.mult, OP.add),
              (sp1, sp3), (sp2,), tagp)
            S(lambda e: nc.scalar.activation(sp2[:, :W], sp2[:, :W], AF.Exp), (sp2,), (sp2,), tagp)
            S(lambda e: nc.scalar.activation(sp2[:, :W], sp2[:, :W], AF.Ln, bias=1.0), (sp2,), (sp2,), tagp)
            V(lambda e: nc.vector.tensor_tensor(sp1[:, :W], sp1[:, :W], sp2[:, :W], OP.add),
              (sp1, sp2), (sp1,), tagp)
            dtd = dt_sb[:, dgo * W:(dgo + 1) * W]
            V(lambda e, dtd=dtd: nc.vector.tensor_scalar(dtd, sp1[:, :W], 0.2, 0.001, OP.min, OP.max),
              (sp1, wch), (dt_sb,), tagp + ":dt")
            xsd = xs[:, dgo * W:(dgo + 1) * W]
            dxd = dtx[:, dgo * W:(dgo + 1) * W]
            V(lambda e, dxd=dxd, dtd=dtd, xsd=xsd: nc.vector.tensor_tensor(dxd, dtd, xsd, OP.mult),
              (dt_sb, xs), (dtx,), tagp + ":dtx")
        pp = ps1
        for kk in range(8):
            lhs = wbcT[:, kk * 2 * N:(kk + 1) * 2 * N]
            rhs = xs[:, kk * W:kk * W + W]
            TMM(pp[0:64, 0:W], lhs, rhs, start=(kk == 0), stop=(kk == 7), tag=tagp + ":bcm")
        V(lambda e: nc.vector.tensor_scalar(bc_fm[:, :W], pp[0:64, :W], bbc[:], None, OP.add),
          (pp, bbc), (bc_fm,), tagp + ":bce")
        dma("scalar", bc_bounce[:], bc_fm[:], tagp + ":bcb")

        for tg in range(NTG):
            xnt = xn_tm[:, tg * D:(tg + 1) * D]
            mk.emit_quant(xnt, xq_tm[:], qscr, D // 16, tagp + ":xq")
            for dg in range(DG):
                TTR(ps_tb[:], xq_tm[:, dg * 128:(dg + 1) * 128], ident_b[:], tagp + ":qtr")
                dcol = xs[:, dg * W + tg * 128:dg * W + (tg + 1) * 128]
                V(lambda e, dcol=dcol: nc.vector.tensor_copy(dcol, ps_tb[:]),
                  (ps_tb,), (xs,), tagp + ":qtre")
        xq_fm = xs

        seg = lambda buf: buf[:].rearrange("p (g b c) -> p g b c", b=B, c=WCB)

        def emit_uc(half, n):
            pi = n % 2
            u_buf = u_bufs[pi]
            u_C = seg(u_buf)[:, :, :, 0:1]
            cidx = (n * 2 + half) * 4 * B
            cv4 = carry[:, cidx:cidx + 4 * B].rearrange("p (g b o) -> p g b o", b=B, o=1)
            S(lambda e, cv4=cv4, u_C=u_C: nc.scalar.activation(u_C, cv4, AF.Copy),
              ((carry, "carry%d" % cidx),), ((u_bufs[pi], "uc%d" % pi),), tagp + ":uc")

        def d_pre(half):
            G(lambda e: nc.gpsimd.memset(Yacc[:], 0.0), (), (Yacc,), tagp + ":ya0")
            emit_uc(half, 0)
            emit_uc(half, 1)

        def d_n(half, n):
            pi = n % 2
            a_buf, V_buf, Brep, Crep = a_bufs[pi], V_bufs[pi], Breps[pi], Creps[pi]
            u_buf = u_bufs[pi]
            ukm = ("um%d" % pi)
            ukc = ("uc%d" % pi)
            a_T = seg(a_buf)[:, :, :, 1:WCB]
            u_T = seg(u_buf)[:, :, :, 1:WCB]
            s_T = u_T
            s_L = seg(u_buf)[:, :, :, T:T + 1]
            cidx = (n * 2 + half) * 4 * B
            cv4 = carry[:, cidx:cidx + 4 * B].rearrange("p (g b o) -> p g b o", b=B, o=1)
            dma("sync", Brep[:], bc_bounce[n:n + 1, :].broadcast_to([128, W]), tagp + ":Br")
            dma("sync", Crep[:], bc_bounce[N + n:N + n + 1, :].broadcast_to([128, W]), tagp + ":Cr")
            for dgi in range(4):
                dg = half * 4 + dgi
                dtd = dt_sb[:, dg * W:(dg + 1) * W].rearrange("p (b t) -> p b t", t=T)
                aT1 = a_T[:, dgi]
                S(lambda e, dtd=dtd, dg=dg, n=n, aT1=aT1:
                  nc.scalar.activation(aT1, dtd, AF.Exp, scale=A_sb[:, dg * N + n:dg * N + n + 1]),
                  (dt_sb, A_sb), (a_buf,), tagp + ":a")
            dxh = dtx[:, half * 4 * W:(half * 4 + 4) * W].rearrange("p (g b t) -> p g b t", b=B, t=T)
            br4 = Brep[:].rearrange("p (b t) -> p b t", t=T).unsqueeze(1).broadcast_to([128, 4, B, T])
            V(lambda e, dxh=dxh, br4=br4, u_T=u_T: nc.vector.tensor_tensor(u_T, dxh, br4, OP.mult),
              (dtx, Brep), ((u_buf, ukm),), tagp + ":u")
            V(lambda e, a_buf=a_buf, u_buf=u_buf: nc.vector.tensor_tensor_scan(u_buf[:], a_buf[:], u_buf[:], 0.0, OP.mult, OP.add),
              (a_buf, (u_buf, ukm), (u_buf, ukc)), ((u_buf, ukm), (u_buf, ukc)), tagp + ":tts")
            S(lambda e, cv4=cv4, s_L=s_L: nc.scalar.activation(cv4, s_L, AF.Copy),
              (((u_buf, ukm)),), ((carry, "carry%d" % cidx),), tagp + ":cs")
            if n + 2 < N:
                emit_uc(half, n + 2)
            cr4 = Crep[:].rearrange("p (b t) -> p b t", t=T).unsqueeze(1).broadcast_to([128, 4, B, T])
            Vv = V_buf[:].rearrange("p (g b t) -> p g b t", b=B, t=T)
            V(lambda e, cr4=cr4, Vv=Vv, s_T=s_T: nc.vector.tensor_tensor(Vv, s_T, cr4, OP.mult),
              (((u_buf, ukm)), Crep), (V_buf,), tagp + ":V")
            G(lambda e, V_buf=V_buf: nc.gpsimd.tensor_tensor(Yacc[:], Yacc[:], V_buf[:], OP.add),
              (Yacc, V_buf), (Yacc,), tagp + ":ya")

        def d_post(half):
            for dgi in range(4):
                dg = half * 4 + dgi
                yv = Yacc[:, dgi * W:(dgi + 1) * W]
                szd = sz[:, dg * W:(dg + 1) * W]
                V(lambda e, yv=yv, szd=szd: nc.vector.tensor_tensor(szd, yv, szd, OP.mult),
                  (Yacc, sz), (sz,), tagp + ":y")

        d_units = []
        for half in range(2):
            d_units.append(lambda half=half: d_pre(half))
            for n in range(N):
                d_units.append(lambda half=half, n=n: d_n(half, n))
            d_units.append(lambda half=half: d_post(half))

        def f_load(ci):
            ic0, icw = CHUNKS[ci]
            gsrc = wq_gT_d[:, ic0:ic0 + icw].rearrange("(kk p) c -> p kk c", p=128)
            dma("scalar", wg_ch[:].rearrange("p (kk c) -> p kk c", c=ICW)[:, :, :icw], gsrc, tagp + ":wgl")
            usrc = wq_uT_d[:, ic0:ic0 + icw].rearrange("(kk p) c -> p kk c", p=128)
            dma("scalar", wu_ch[:].rearrange("p (kk c) -> p kk c", c=ICW)[:, :, :icw], usrc, tagp + ":wul")
            dsrc = wq_dT_d[ic0:ic0 + icw, :].rearrange("(it p) c -> p it c", p=128)
            dma("scalar", wd_ch[:, :(icw // 128) * D].rearrange("p (it c) -> p it c", c=D), dsrc, tagp + ":wdl")

        def f_dacc0():
            G(lambda e: nc.gpsimd.memset(dacc[:], 0.0), (), (dacc,), tagp + ":da0")

        def f_mm(ci, tg):
            u = (ci * NTG + tg) % 2
            pg, py = (ps_g, ps_y) if u == 0 else (ps_g2, ps_y2)
            for kk in range(8):
                lhs = xq_fm[:, kk * W + tg * 128:kk * W + (tg + 1) * 128]
                TMM(pg[:, :ICW], lhs, wg_ch[:, kk * ICW:(kk + 1) * ICW],
                    start=(kk == 0), stop=(kk == 7), tag=tagp + ":gm")
            for kk in range(8):
                lhs = xq_fm[:, kk * W + tg * 128:kk * W + (tg + 1) * 128]
                TMM(py[:, :ICW], lhs, wu_ch[:, kk * ICW:(kk + 1) * ICW],
                    start=(kk == 0), stop=(kk == 7), tag=tagp + ":um")

        def f_post(ci, tg):
            ic0, icw = CHUNKS[ci]
            nit = icw // 128
            u = (ci * NTG + tg) % 2
            pg, py = (ps_g, ps_y) if u == 0 else (ps_g2, ps_y2)
            S(lambda e, icw=icw, pg=pg: nc.scalar.activation(hs_sb[:, :icw], pg[:, :icw], AF.Silu),
              (pg,), (hs_sb,), tagp + ":hs")
            V(lambda e, icw=icw, py=py: nc.vector.tensor_tensor(h_sb[:, :icw], hs_sb[:, :icw], py[:, :icw], OP.mult),
              (hs_sb, py, wg_ch, wu_ch), (h_sb,), tagp + ":h")
            mk.emit_quant(h_sb[:, :icw], hq_tm[:, :icw], qscr, icw // 16, tagp + ":hq")
            for it in range(nit):
                TTR(ps_tb[:], hq_tm[:, it * 128:(it + 1) * 128], ident_b[:], tagp + ":htr")
                dcol = hq_fm[:, it * 128:(it + 1) * 128]
                V(lambda e, dcol=dcol: nc.vector.tensor_copy(dcol, ps_tb[:]),
                  (ps_tb,), (hq_fm,), tagp + ":htre")
            for jc in range(2):
                for it in range(nit):
                    TMM(ps_o[:, 0:512], hq_fm[:, it * 128:(it + 1) * 128],
                        wd_ch[:, it * D + jc * 512:it * D + (jc + 1) * 512],
                        start=(it == 0), stop=(it == nit - 1), tag=tagp + ":dm")
                dv = dacc[:, tg * D + jc * 512:tg * D + (jc + 1) * 512]
                V(lambda e, dv=dv: nc.vector.tensor_tensor(dv, dv, ps_o[:, 0:512], OP.add),
                  (dacc, ps_o, wd_ch), (dacc,), tagp + ":dacc")

        def f_woutl():
            wsrc = woutT_d[:, :].rearrange("(kk p) c -> p kk c", p=128)
            dma("scalar", wout_ph[:].rearrange("p (kk c) -> p kk c", c=D), wsrc, tagp + ":wol")

        f_units = [lambda: f_load(0), f_dacc0, f_woutl]
        for ci in range(NCH):
            pend = None
            for tg in range(NTG):
                f_units.append(lambda ci=ci, tg=tg: f_mm(ci, tg))
                if pend is not None:
                    f_units.append(lambda p=pend: f_post(*p))
                pend = (ci, tg)
            f_units.append(lambda p=pend: f_post(*p))
            if ci + 1 < NCH:
                f_units.append(lambda ci=ci: f_load(ci + 1))

        INTERLEAVE = True
        if INTERLEAVE:
            npre = 3
            for k in range(npre):
                f_units[k]()
            f_done = npre
            nf, nd = len(f_units), len(d_units)
            for di, du in enumerate(d_units):
                du()
                want = npre + (di + 1) * (nf - npre) // nd
                while f_done < want:
                    f_units[f_done]()
                    f_done += 1
            while f_done < nf:
                f_units[f_done]()
                f_done += 1
        else:
            for du in d_units:
                du()
            for fu in f_units:
                fu()

        for tg in range(NTG):
            b, j = tg // TPB, tg % TPB
            for jc in range(2):
                for kk in range(DG):
                    lhs = y_fm[:, kk * W + tg * 128:kk * W + (tg + 1) * 128]
                    TMM(ps_o[:, 0:512], lhs,
                        wout_ph[:, kk * D + jc * 512:kk * D + (jc + 1) * 512],
                        start=(kk == 0), stop=(kk == 7), tag=tagp + ":om")
                ob_h = ob_rep[:, jc * 512:(jc + 1) * 512]
                at_h = attn_sb[:, jc * 512:(jc + 1) * 512]
                V(lambda e, ob_h=ob_h, at_h=at_h: nc.vector.tensor_tensor(at_h, ps_o[:, 0:512], ob_h, OP.add),
                  (ps_o, ob_rep, wout_ph), (attn_sb,), tagp + ":ob")
            xnt = xn_tm[:, tg * D:(tg + 1) * D]
            V(lambda e, xnt=xnt: nc.vector.tensor_tensor(attn_sb[:], attn_sb[:], xnt, OP.add),
              (attn_sb, xn_tm), (attn_sb,), tagp + ":resm")
            mk.emit_ln(attn_sb[:], attn_sb[:], lnscr, gamma=g_rep[:], beta=b_rep[:], tag=tagp + ":mln")
            ta = t0 + j * 128
            dma("sync", xres[:], x_in[b, ta:ta + 128, :], tagp + ":xr")
            V(lambda e: nc.vector.tensor_tensor(o_sb[:], attn_sb[:], xres[:], OP.add),
              (attn_sb, q_p2), (o_sb,), tagp + ":fin1")
            dv = dacc[:, tg * D:(tg + 1) * D]
            V(lambda e, dv=dv: nc.vector.tensor_tensor(o_sb[:], o_sb[:], dv, OP.add),
              (o_sb, dacc), (o_sb,), tagp + ":fin2")
            dma("scalar", out_t[b, ta:ta + 128, :], o_sb[:], tagp + ":out")

    V(lambda e: nc.vector.engine_nop(), (out_t,), (), "final")

    from contextlib import ExitStack
    with ExitStack() as es:
        _sem_n = [0]

        def sem_factory(key):
            _sem_n[0] += 1
            return es.enter_context(nc.semaphore(f"sm{_sem_n[0]}"))

        block = es.enter_context(nc.Block())
        p.build(block, sem_factory)
    nc.finalize()
    return nc


_NC_CACHE = {}


def _get_nc(B, L, T):
    key = (B, L, T)
    if key not in _NC_CACHE:
        _NC_CACHE[key] = build_core_program(B, L, T)
    return _NC_CACHE[key]


def make_in_map(inputs, xshard):
    c = lambda a: np.ascontiguousarray(np.asarray(a), dtype=np.float32)
    return dict(
        x=c(xshard),
        w_inT=c(np.asarray(inputs["L0_attn_in_proj_w"]).T),
        b_in=c(inputs["L0_attn_in_proj_b"]),
        conv_w=c(np.asarray(inputs["L0_attn_conv_w"]).reshape(D, K4)),
        conv_b=c(inputs["L0_attn_conv_b"]),
        A_log=c(inputs["L0_attn_A_log"]),
        w_bcT=c(np.concatenate([np.asarray(inputs["L0_attn_B_w"]).T,
                                np.asarray(inputs["L0_attn_C_w"]).T], axis=1)),
        b_bc=c(np.concatenate([np.asarray(inputs["L0_attn_B_b"]),
                               np.asarray(inputs["L0_attn_C_b"])])),
        w_dtT=c(np.asarray(inputs["L0_attn_dt_w"]).T),
        b_dt=c(inputs["L0_attn_dt_b"]),
        w_outT=c(np.asarray(inputs["L0_attn_out_w"]).T),
        b_out=c(inputs["L0_attn_out_b"]),
        ln_g=c(inputs["L0_attn_ln_g"]),
        ln_b=c(inputs["L0_attn_ln_beta"]),
        w_g=c(inputs["L0_ffn_w_gate"]),
        w_u=c(inputs["L0_ffn_w_up"]),
        w_d=c(inputs["L0_ffn_w_down"]),
        ident=np.eye(128, dtype=np.float32),
    )


def kernel(**inputs):
    x = np.asarray(inputs["L0_attn_x"])
    NB = x.shape[0]
    ncores = 8
    bpc = NB // ncores
    nc = _get_nc(bpc, x.shape[1], 128)
    in_maps = [make_in_map(inputs, x[i * bpc:(i + 1) * bpc]) for i in range(ncores)]
    res = run_bass_kernel_spmd(nc, in_maps, core_ids=list(range(ncores)))
    out = np.concatenate([np.asarray(r["out"]) for r in res.results], axis=0)
    return out.astype(np.float32)
